# version 15 (final)
# speedup vs baseline: 1.3347x; 1.1658x over previous
import math
import numpy as np
from contextlib import ExitStack

import concourse.bass as bass
import concourse.mybir as mybir

F32 = mybir.dt.float32
BF16 = mybir.dt.bfloat16
U32 = mybir.dt.uint32
AF = mybir.ActivationFunctionType
ALU = mybir.AluOpType
AX = mybir.AxisListType
IOA = bass.IndirectOffsetOnAxis

D = 2048
H = 16
HKV = 4
DH = 128
E = 64
TOPK = 6
DCH = D // 128
ESTRIDE = 1024


class Cfg:
    def __init__(self, S=4096, ST=512, CAP=1024, fp32=False):
        self.S = S
        self.ST = ST
        self.CAP = CAP
        self.fp32 = fp32
        self.NSUP = S // ST
        self.NT = S // 128
        self.LAM = [2.0 ** (-(h + 1) / 2.0) for h in range(H)]


class Sem:
    def __init__(self, ctx, nc, name):
        self.h = ctx.enter_context(nc.semaphore(name))
        self.n = 0

    def inc(self, instr, k=1):
        instr.then_inc(self.h, k)
        self.n += k
        return self.n


def wait(eng, sem, val=None):
    eng.wait_ge(sem.h, sem.n if val is None else val)



def pack_l1_inputs(cfg, x_c, core_id, wq, wk, wv, wo, w_gate):
    S, ST, NSUP = cfg.S, cfg.ST, cfg.NSUP
    xT = np.ascontiguousarray(x_c.T)
    xTp = np.ascontiguousarray(xT.reshape(DCH, 128, NSUP, ST).transpose(2, 1, 0, 3))
    wqTp = np.ascontiguousarray(wq.T.reshape(DCH, 128, 16, 128))
    wkTp = np.ascontiguousarray(wk.T.reshape(DCH, 128, 4, 128))
    wvTp = np.ascontiguousarray(wv.T.reshape(DCH, 128, 512))
    woTp = np.ascontiguousarray(wo.T.reshape(DCH, 128, 2048))
    wgp = np.ascontiguousarray(w_gate.reshape(DCH, 128, E))
    tokid = (core_id * S + np.arange(128, dtype=np.float32)).reshape(128, 1)
    return {
        "xTp": xTp, "x": np.ascontiguousarray(x_c),
        "wqTp": wqTp, "wkTp": wkTp, "wvTp": wvTp, "woTp": woTp, "wgp": wgp,
        "tokid": np.ascontiguousarray(tokid),
    }



def build_l1(cfg, dbg=False):
    global BF16
    BF16_saved = BF16
    if cfg.fp32:
        BF16 = F32
    try:
        return _build_l1(cfg, dbg)
    finally:
        BF16 = BF16_saved


def _build_l1(cfg, dbg=False):
    S, ST, NSUP, NT, CAP = cfg.S, cfg.ST, cfg.NSUP, cfg.NT, cfg.CAP
    NSUB = ST // 128
    NPROJ = 16 + 4 + NSUB
    nc = bass.Bass("TRN2", target_bir_lowering=False)

    xTp = nc.dram_tensor("xTp", [NSUP, 128, DCH, ST], F32, kind="ExternalInput").ap()
    x_in = nc.dram_tensor("x", [S, D], F32, kind="ExternalInput").ap()
    wqTp = nc.dram_tensor("wqTp", [DCH, 128, 16, 128], F32, kind="ExternalInput").ap()
    wkTp = nc.dram_tensor("wkTp", [DCH, 128, 4, 128], F32, kind="ExternalInput").ap()
    wvTp = nc.dram_tensor("wvTp", [DCH, 128, 512], F32, kind="ExternalInput").ap()
    woTp = nc.dram_tensor("woTp", [DCH, 128, 2048], F32, kind="ExternalInput").ap()
    wgp = nc.dram_tensor("wgp", [DCH, 128, E], F32, kind="ExternalInput").ap()
    tokid_in = nc.dram_tensor("tokid", [128, 1], F32, kind="ExternalInput").ap()

    y_out = nc.dram_tensor("y", [S, D], BF16, kind="ExternalOutput").ap()
    yTp = nc.dram_tensor("yTp", [S // 256, 128, 16 * 256], BF16, kind="ExternalOutput").ap()
    x2_out = nc.dram_tensor("x2", [S, D], F32, kind="ExternalOutput").ap()
    idx_out = nc.dram_tensor("idxo", [S, TOPK], F32, kind="ExternalOutput").ap()
    wgt_out = nc.dram_tensor("wgto", [S, TOPK], F32, kind="ExternalOutput").ap()
    pos_out = nc.dram_tensor("poso", [S, TOPK], F32, kind="ExternalOutput").ap()
    cnt_out = nc.dram_tensor("cnto", [1, E], F32, kind="ExternalOutput").ap()
    tab_out = nc.dram_tensor("tabo", [E * ESTRIDE, 2], F32, kind="ExternalOutput").ap()

    kind_dbg = "ExternalOutput" if dbg else "Internal"
    if dbg:
        costab_d = nc.dram_tensor("costab_d", [128, S], BF16, kind="ExternalOutput").ap()
        sintab_d = nc.dram_tensor("sintab_d", [128, S], BF16, kind="ExternalOutput").ap()
        pswap_d = nc.dram_tensor("pswap_d", [128, 128], BF16, kind="ExternalOutput").ap()
    QTp = nc.dram_tensor("QTp", [NT, 128, 16 * 128], BF16, kind=kind_dbg).ap()
    KTp = nc.dram_tensor("KTp", [NT, 128, 4 * 128], BF16, kind=kind_dbg).ap()
    Vr = nc.dram_tensor("Vr", [S, 512], BF16, kind=kind_dbg).ap()

    ctx = ExitStack()
    def sbt(name, shape, dt=F32):
        return ctx.enter_context(nc.sbuf_tensor(name, shape, dt))
    def pst(name, shape, dt=F32):
        return ctx.enter_context(nc.psum_tensor(name, shape, dt))

    ones_col = sbt("ones_col", [128, 1], BF16)
    ones_row = sbt("ones_row", [1, 128], BF16)
    ones_row32 = sbt("ones_row32", [1, 128], F32)
    ident = sbt("ident", [128, 128], BF16)
    iotaE = sbt("iotaE", [128, E], F32)
    tokid_col = sbt("tokid_col", [128, 1], F32)
    LtriT = sbt("LtriT", [128, 128], BF16)
    pswap = sbt("pswap", [128, 128], BF16)
    decayT = sbt("decayT", [128, H * 128], BF16)
    qdec_b = sbt("qdec_b", [128, H * 128], BF16)
    kdec_c = sbt("kdec_c", [128, H], F32)
    half_neg = sbt("half_neg", [128, 1], F32)
    eps_t = sbt("eps_t", [128, 1], F32)
    czero = sbt("czero", [128, 256], F32)

    sems = {}
    def S_(name):
        sems[name] = Sem(ctx, nc, name)
        return sems[name]
    s_cb = [S_(f"cb{i}") for i in range(5)]
    s_ldx = S_("ldx"); s_ldw = S_("ldw"); s_ldt = S_("ldt")
    s_sto = S_("sto")
    s_gp = S_("gpd")
    s_pe = S_("pe"); s_dve = S_("dve"); s_act = S_("act"); s_pool = S_("pool")
    s_scat = S_("scat")
    s_pe2 = S_("pe2")

    with nc.Block() as block:
        patx = ExitStack()
        wq_sb = patx.enter_context(nc.sbuf_tensor("wq_sb", [128, DCH * 16 * 128], BF16))
        wk_sb = patx.enter_context(nc.sbuf_tensor("wk_sb", [128, DCH * 4 * 128], BF16))
        wv_sb = patx.enter_context(nc.sbuf_tensor("wv_sb", [128, DCH * 512], BF16))
        costab = patx.enter_context(nc.sbuf_tensor("costab", [128, S], BF16))
        sintab = patx.enter_context(nc.sbuf_tensor("sintab", [128, S], BF16))
        cctx = ExitStack()
        c_a = cctx.enter_context(nc.sbuf_tensor("c_a", [128, S], F32))
        c_b = cctx.enter_context(nc.sbuf_tensor("c_b", [128, S], F32))
        c_u = cctx.enter_context(nc.sbuf_tensor("c_u", [128, S], U32))
        c_d = cctx.enter_context(nc.sbuf_tensor("c_d", [128, 4096 + 2 * H], F32))

        @block.gpsimd
        def _(g):
            g.memset(ones_col[:], 1.0)
            g.memset(ones_row[:], 1.0)
            g.memset(ones_row32[:], 1.0)
            g.memset(czero[:], 0.0)
            g.iota(out=c_d[:, 0:128], pattern=[[-1, 128]], base=0, channel_multiplier=1,
                   allow_small_or_imprecise_dtypes=True)
            g.iota(out=iotaE[:], pattern=[[1, E]], base=0, channel_multiplier=0,
                   allow_small_or_imprecise_dtypes=True)
            g.iota(out=c_a[:], pattern=[[1, S]], base=0, channel_multiplier=0,
                   allow_small_or_imprecise_dtypes=True)
            g.iota(out=c_b[:, 0:1], pattern=[[0, 1]], base=0, channel_multiplier=1,
                   allow_small_or_imprecise_dtypes=True)
            g.drain()
            s_pool.inc(g.engine_nop())

        @block.vector
        def _(v):
            wait(v, s_pool)
            g128 = c_d[:, 0:128]
            v.tensor_scalar(out=c_d[:, 128:256], in0=g128, scalar1=0.0, scalar2=None,
                            op0=ALU.is_equal)
            v.tensor_scalar(out=c_d[:, 256:384], in0=g128, scalar1=-0.5, scalar2=None,
                            op0=ALU.is_lt)
            v.tensor_scalar(out=c_d[:, 384:512], in0=g128, scalar1=64.0, scalar2=None,
                            op0=ALU.is_equal)
            v.tensor_scalar(out=c_d[:, 512:640], in0=g128, scalar1=-64.0, scalar2=None,
                            op0=ALU.is_equal)
            v.drain()
            v.tensor_copy(out=ident[:], in_=c_d[:, 128:256])
            v.tensor_copy(out=LtriT[:], in_=c_d[:, 256:384])
            v.tensor_sub(out=c_d[:, 384:512], in0=c_d[:, 512:640], in1=c_d[:, 384:512])
            v.drain()
            v.tensor_copy(out=pswap[:], in_=c_d[:, 384:512])
            v.drain()
            s_dve.inc(v.engine_nop())

        @block.vector
        def _(v):
            v.memset(half_neg[:], -0.5)
            v.memset(eps_t[:], 1e-5)
            wait(v, s_pool)
            v.drain()
            v.tensor_scalar(out=c_b[:, 1:2], in0=c_b[:, 0:1], scalar1=63.5, scalar2=None,
                            op0=ALU.is_gt)
            v.drain()
            v.scalar_tensor_tensor(out=c_b[:, 2:3], in0=c_b[:, 1:2], scalar=-64.0,
                                   in1=c_b[:, 0:1], op0=ALU.mult, op1=ALU.add)
            v.drain()
            v.tensor_scalar(out=c_b[:, 3:4], in0=c_b[:, 2:3],
                            scalar1=-2.0 * math.log(10000.0) / 128.0, scalar2=None,
                            op0=ALU.mult)
            v.drain()
            s_dve.inc(v.engine_nop())

        @block.scalar
        def _(s):
            wait(s, s_dve)
            s.activation(out=c_b[:, 4:5], in_=c_b[:, 3:4], func=AF.Exp)
            s.drain()
            s_act.inc(s.nop())

        @block.vector
        def _(v):
            wait(v, s_act)
            v.tensor_scalar(out=c_a[:], in0=c_a[:], scalar1=c_b[:, 4:5], scalar2=None,
                            op0=ALU.mult)
            v.drain()
            s_dve.inc(v.engine_nop())

        def sincos(tab, shift):
            @block.vector
            def _(v):
                wait(v, s_dve); wait(v, s_act)
                v.tensor_scalar(out=c_b[:], in0=c_a[:],
                                scalar1=shift, scalar2=None, op0=ALU.add)
                v.drain()
                v.tensor_scalar(out=c_u[:].bitcast(F32) if False else c_b[:], in0=c_b[:],
                                scalar1=1.0 / (2.0 * math.pi), scalar2=0.5,
                                op0=ALU.mult, op1=ALU.add)
                v.drain()
                v.tensor_copy(out=c_u[:], in_=c_b[:])
                v.drain()
                v.tensor_copy(out=c_b[:], in_=c_u[:])
                v.drain()
                v.tensor_scalar(out=c_b[:], in0=c_b[:], scalar1=-2.0 * math.pi,
                                scalar2=None, op0=ALU.mult)
                v.drain()
                v.tensor_add(out=c_b[:], in0=c_b[:], in1=c_a[:])
                v.drain()
                if shift != 0.0:
                    v.tensor_scalar(out=c_b[:], in0=c_b[:], scalar1=shift, scalar2=None,
                                    op0=ALU.add)
                    v.drain()
                v.tensor_scalar(out=c_d[:, 0:S], in0=c_b[:], scalar1=-math.pi,
                                scalar2=None, op0=ALU.is_lt)
                v.drain()
                v.scalar_tensor_tensor(out=c_b[:], in0=c_d[:, 0:S],
                                       scalar=2.0 * math.pi, in1=c_b[:],
                                       op0=ALU.mult, op1=ALU.add)
                v.drain()
                v.tensor_scalar(out=c_d[:, 0:S], in0=c_b[:], scalar1=math.pi,
                                scalar2=None, op0=ALU.is_gt)
                v.drain()
                v.scalar_tensor_tensor(out=c_b[:], in0=c_d[:, 0:S],
                                       scalar=-2.0 * math.pi, in1=c_b[:],
                                       op0=ALU.mult, op1=ALU.add)
                v.drain()
                s_dve.inc(v.engine_nop())

            @block.scalar
            def _(s):
                wait(s, s_dve)
                s.activation(out=tab[:], in_=c_b[:], func=AF.Sin)
                s.drain()
                s_act.inc(s.nop())

        sincos(sintab, 0.0)
        sincos(costab, math.pi / 2.0)

        @block.gpsimd
        def _(g):
            wait(g, s_act)
            g.iota(out=c_a[:, 0:128], pattern=[[1, 128]], base=0, channel_multiplier=-1,
                   allow_small_or_imprecise_dtypes=True)
            g.iota(out=c_a[:, 128:256], pattern=[[1, 128]], base=1, channel_multiplier=0,
                   allow_small_or_imprecise_dtypes=True)
            g.iota(out=c_a[:, 256:257], pattern=[[0, 1]], base=127, channel_multiplier=-1,
                   allow_small_or_imprecise_dtypes=True)
            g.drain()
            s_pool.inc(g.engine_nop())

        @block.vector
        def _(v):
            wait(v, s_pool)
            for h in range(H):
                lam = cfg.LAM[h]
                v.tensor_scalar(out=c_d[:, h * 128:(h + 1) * 128], in0=c_a[:, 0:128],
                                scalar1=-lam, scalar2=None, op0=ALU.mult)
                v.tensor_scalar(out=c_d[:, 2048 + h * 128:2048 + (h + 1) * 128],
                                in0=c_a[:, 128:256], scalar1=-lam, scalar2=None, op0=ALU.mult)
                v.tensor_scalar(out=c_d[:, 4096 + h:4097 + h], in0=c_a[:, 256:257],
                                scalar1=-lam, scalar2=None, op0=ALU.mult)
            v.drain()
            v.tensor_scalar(out=c_d[:, 0:2048], in0=c_d[:, 0:2048], scalar1=0.0,
                            scalar2=None, op0=ALU.min)
            v.drain()
            s_dve.inc(v.engine_nop())

        @block.scalar
        def _(s):
            wait(s, s_dve)
            s.activation(out=decayT[:], in_=c_d[:, 0:2048], func=AF.Exp)
            s.activation(out=qdec_b[:], in_=c_d[:, 2048:4096], func=AF.Exp)
            s.activation(out=kdec_c[:], in_=c_d[:, 4096:4096 + H], func=AF.Exp)
            s.drain()
            s_act.inc(s.nop())

        @block.vector
        def _(v):
            wait(v, s_act)
            v.tensor_scalar(out=c_b[:, 0:128], in0=c_a[:, 0:128], scalar1=-0.5,
                            scalar2=None, op0=ALU.is_gt)
            v.drain()
            for h in range(H):
                v.tensor_mul(out=decayT[:, h * 128:(h + 1) * 128],
                             in0=decayT[:, h * 128:(h + 1) * 128], in1=c_b[:, 0:128])
            v.drain()
            s_dve.inc(v.engine_nop())

        @block.gpsimd
        def _(g):
            wait(g, s_dve)
            s_pool.inc(g.engine_nop())

        @block.gpsimd
        def _(g):
            wait(g, s_pool, 1)
            nz = (E * ESTRIDE * 2) // (128 * 256)
            tz = tab_out.rearrange("(a p qq) b -> p a (qq b)", p=128, qq=128)
            for i in range(nz):
                s_gp.inc(g.dma_start(out=tz[:, i, :], in_=czero[:]), 16)

        @block.sync
        def _(sy):
            s_ldt.inc(sy.dma_start(out=tokid_col[:], in_=tokid_in), 16)
        if dbg:
            @block.scalar
            def _(s):
                for cb in s_cb[:2]:
                    wait(s, cb)
                s_sto.inc(s.dma_start(out=costab_d, in_=costab[:]), 16)
                s_sto.inc(s.dma_start(out=sintab_d, in_=sintab[:]), 16)
                s_sto.inc(s.dma_start(out=pswap_d, in_=pswap[:]), 16)

        @block.gpsimd
        def _(g):
            for c in range(DCH):
                s_gp.inc(g.dma_start(out=wq_sb[:, c * 2048:(c + 1) * 2048],
                                     in_=wqTp[c].rearrange("p a b -> p (a b)")), 16)
                s_gp.inc(g.dma_start(out=wk_sb[:, c * 512:(c + 1) * 512],
                                     in_=wkTp[c].rearrange("p a b -> p (a b)")), 16)
                s_gp.inc(g.dma_start(out=wv_sb[:, c * 512:(c + 1) * 512],
                                     in_=wvTp[c]), 16)
        GP_AFTER_W = s_gp.n

        @block.gpsimd
        def _(g):
            s_cb[0].inc(g.engine_nop())
        @block.vector
        def _(v):
            s_cb[1].inc(v.engine_nop())
        @block.scalar
        def _(s):
            s_cb[2].inc(s.nop())
        @block.tensor
        def _(t):
            s_cb[3].inc(t.nop())
        @block.sync
        def _(sy):
            s_cb[4].inc(sy.nop())

        cctx.close()

        atx = ExitStack()
        def sbt_a(name, shape, dt=F32):
            return atx.enter_context(nc.sbuf_tensor(name, shape, dt))
        xt = sbt_a("xt", [128, DCH * ST], BF16)
        sq_s = sbt_a("sq_s", [128, 8 * ST], BF16)
        xln = sbt_a("xln", [128, DCH * ST], BF16)
        t_s = sbt_a("t_s", [128, 2 * ST], F32)
        m_sb = sbt_a("m_sb", [128, ST], F32)
        r_sb = sbt_a("r_sb", [128, ST], F32)
        rowm = sbt_a("rowm", [1, ST], F32)

        rowv = sbt_a("rowv", [1, ST], F32)
        rowt = sbt_a("rowt", [1, ST], F32)
        rowr = sbt_a("rowr", [1, ST], F32)

        cc_s = sbt_a("cc_s", [128, 2 * ST], BF16)
        ss_s = sbt_a("ss_s", [128, 2 * ST], BF16)
        qt_out = sbt_a("qt_out", [128, 16 * ST], BF16)
        kt_out = sbt_a("kt_out", [128, 4 * ST], BF16)
        v_out = sbt_a("v_out", [128, NSUB * 512], BF16)

        actx = atx
        ps_sum = actx.enter_context(nc.psum_tensor("ps_sum", [1, ST], F32))
        ps_sq = actx.enter_context(nc.psum_tensor("ps_sq", [1, ST], F32))
        ps_mb = actx.enter_context(nc.psum_tensor("ps_mb", [128, ST], F32))
        ps_rb = actx.enter_context(nc.psum_tensor("ps_rb", [128, ST], F32))
        ps_proj = [actx.enter_context(nc.psum_tensor(f"ps_p{i}", [128, ST], F32))
                   for i in range(3)]
        ps_rot = actx.enter_context(nc.psum_tensor("ps_rot", [128, ST], F32))

        bank_free = [None, None, None]
        stats_pe_done = 0
        stats_free = None
        bcast_free = None
        xt_free = None

        for sg in range(NSUP):
            base_t = sg * ST

            @block.gpsimd
            def _(g, sg=sg):
                if sg == 0:
                    for cb in s_cb:
                        wait(g, cb)
                else:
                    wait(g, *xt_free)
                s_ldx.inc(g.dma_start(out=xt[:], in_=xTp[sg].rearrange("p a b -> p (a b)")), 16)

            pe_stats_base = s_pe.n
            act_sq_base = s_act.n
            @block.scalar
            def _(s, sg=sg):
                wait(s, s_ldx)
                if sg == 0:
                    for cb in s_cb:
                        wait(s, cb)
                else:
                    wait(s, s_pe, stats_pe_done)
                for c in range(DCH):
                    if c >= 8:
                        wait(s, s_pe, pe_stats_base + (c - 8) + 1)
                    s_act.inc(s.activation(out=sq_s[:, (c % 8) * ST:(c % 8 + 1) * ST],
                                           in_=xt[:, c * ST:(c + 1) * ST], func=AF.Square))

            @block.tensor
            def _(t, sg=sg):
                if sg == 0:
                    for cb in s_cb:
                        wait(t, cb)
                if stats_free is not None:
                    wait(t, *stats_free)
                for c in range(DCH):
                    wait(t, s_act, act_sq_base + c + 1)
                    t.matmul(ps_sum[:], ones_col[:], xt[:, c * ST:(c + 1) * ST],
                             start=(c == 0), stop=(c == DCH - 1))
                    s_pe.inc(t.matmul(ps_sq[:], ones_col[:],
                                      sq_s[:, (c % 8) * ST:(c % 8 + 1) * ST],
                                      start=(c == 0), stop=(c == DCH - 1)))
            stats_pe_done = s_pe.n

            @block.vector
            def _(v, sg=sg):
                wait(v, s_pe)
                v.tensor_scalar(out=rowm[:], in0=ps_sum[:], scalar1=1.0 / D, scalar2=None,
                                op0=ALU.mult)
                v.tensor_scalar(out=rowv[:], in0=ps_sq[:], scalar1=1.0 / D, scalar2=None,
                                op0=ALU.mult)
                v.drain()
                v.tensor_mul(out=rowt[:], in0=rowm[:], in1=rowm[:])
                v.drain()
                v.tensor_sub(out=rowv[:], in0=rowv[:], in1=rowt[:])
                v.drain()
                s_dve.inc(v.engine_nop())
            stats_free = (s_dve, s_dve.n)

            @block.scalar
            def _(s, sg=sg):
                wait(s, s_dve)
                s.activation(out=rowr[:], in_=rowv[:], func=AF.Ln, bias=eps_t[0:1, 0:1])
                s.drain()
                s.activation(out=rowr[:], in_=rowr[:], func=AF.Exp, scale=half_neg[0:1, 0:1])
                s.drain()
                s_act.inc(s.nop())

            @block.tensor
            def _(t, sg=sg):
                wait(t, s_act)
                if bcast_free is not None:
                    wait(t, *bcast_free)
                t.matmul(ps_mb[:], ones_row32[:], rowm[:], start=True, stop=True)
                s_pe.inc(t.matmul(ps_rb[:], ones_row32[:], rowr[:], start=True, stop=True))

            @block.vector
            def _(v, sg=sg):
                wait(v, s_pe)
                v.tensor_copy(out=m_sb[:], in_=ps_mb[:])
                v.tensor_copy(out=r_sb[:], in_=ps_rb[:])
                v.drain()
                s_dve.inc(v.engine_nop())
            bcast_free = (s_dve, s_dve.n)

            dve_ln_base = s_dve.n
            pool_ln_base = s_pool.n
            @block.vector
            def _(v, sg=sg):
                wait(v, s_dve)
                for c in range(DCH):
                    if c >= 2:
                        wait(v, s_pool, pool_ln_base + (c - 2) + 1)
                    s_dve.inc(v.tensor_sub(out=t_s[:, (c % 2) * ST:(c % 2 + 1) * ST],
                                           in0=xt[:, c * ST:(c + 1) * ST], in1=m_sb[:]))
            @block.gpsimd
            def _(g, sg=sg):
                for c in range(DCH):
                    wait(g, s_dve, dve_ln_base + c + 1)
                    s_pool.inc(g.tensor_mul(out=xln[:, c * ST:(c + 1) * ST],
                                            in0=t_s[:, (c % 2) * ST:(c % 2 + 1) * ST],
                                            in1=r_sb[:]))
            xt_free = (s_dve, s_dve.n)

            pe_proj_base = s_pe.n
            dve_rope_pred = s_dve.n
            act_v_pred = s_act.n
            act_v_base = act_v_pred
            xs_mark = {}
            comb_mark = {}
            rot_mark = {}
            pe_c = s_pe.n
            for o in range(NPROJ):
                pe_c += 1
                if o >= 2 and o - 2 < 20:
                    pe_c += 1
                    rot_mark[o - 2] = pe_c
            dve_c = s_dve.n
            for o in range(22):
                if o >= 2:
                    dve_c += 1
                    comb_mark[o - 2] = dve_c
                if o < 20:
                    dve_c += 1
                    xs_mark[o] = dve_c

            @block.tensor
            def _(t, sg=sg):
                if sg == 0:
                    wait(t, s_gp, GP_AFTER_W)
                for o in range(NPROJ):
                    bank = o % 3
                    if o >= 3:
                        prev = o - 3
                        if prev < 20:
                            wait(t, s_dve, xs_mark[prev])
                        else:
                            wait(t, s_act, act_v_pred + (prev - 20) + 1)
                    elif bank_free[bank] is not None:
                        wait(t, *bank_free[bank])
                    bk = ps_proj[bank]
                    mm = None
                    for c in range(DCH):
                        if o == 0:
                            wait(t, s_pool, pool_ln_base + c + 1)
                        if o < 16:
                            lhs = wq_sb[:, (c * 16 + o) * 128:(c * 16 + o + 1) * 128]
                            mm = t.matmul(bk[:], lhs, xln[:, c * ST:(c + 1) * ST],
                                          start=(c == 0), stop=(c == DCH - 1))
                        elif o < 20:
                            kk = o - 16
                            lhs = wk_sb[:, (c * 4 + kk) * 128:(c * 4 + kk + 1) * 128]
                            mm = t.matmul(bk[:], lhs, xln[:, c * ST:(c + 1) * ST],
                                          start=(c == 0), stop=(c == DCH - 1))
                        else:
                            sub = o - 20
                            mm = t.matmul(bk[:, 0:512],
                                          xln[:, c * ST + sub * 128:c * ST + sub * 128 + 128],
                                          wv_sb[:, c * 512:(c + 1) * 512],
                                          start=(c == 0), stop=(c == DCH - 1))
                    s_pe.inc(mm)
                    if o >= 2 and o - 2 < 20:
                        ro = o - 2
                        wait(t, s_dve, xs_mark[ro])
                        if ro > 0:
                            wait(t, s_dve, comb_mark.get(ro - 1, 0))
                        s_pe.inc(t.matmul(ps_rot[:], pswap[:],
                                          ss_s[:, (ro % 2) * ST:(ro % 2 + 1) * ST],
                                          start=True, stop=True))
            assert s_pe.n == pe_c
            qkv_mark = {}
            pc = pe_proj_base
            for o in range(NPROJ):
                pc += 1
                qkv_mark[o] = pc
                if o >= 2 and o - 2 < 20:
                    pc += 1

            @block.vector
            def _(v, sg=sg):
                for o in range(22):
                    if o >= 2:
                        co = o - 2
                        wait(v, s_pe, rot_mark[co])
                        v.drain()
                        dst = (qt_out[:, co * ST:(co + 1) * ST] if co < 16
                               else kt_out[:, (co - 16) * ST:(co - 15) * ST])
                        s_dve.inc(v.tensor_add(out=dst,
                                               in0=cc_s[:, (co % 2) * ST:(co % 2 + 1) * ST],
                                               in1=ps_rot[:]))
                    if o < 20:
                        wait(v, s_pe, qkv_mark[o])
                        bk = ps_proj[o % 3]
                        sl = o % 2
                        if o >= 2:
                            wait(v, s_pe, rot_mark[o - 2])
                        v.drain()
                        v.tensor_mul(out=cc_s[:, sl * ST:(sl + 1) * ST], in0=bk[:],
                                     in1=costab[:, base_t:base_t + ST])
                        s_dve.inc(v.tensor_mul(out=ss_s[:, sl * ST:(sl + 1) * ST],
                                               in0=bk[:],
                                               in1=sintab[:, base_t:base_t + ST]))
            assert s_dve.n == dve_c
            dve_comb_last = dve_c
            rope_marks = [(xs_mark[o], rot_mark[o], comb_mark[o]) for o in range(20)]

            @block.scalar
            def _(s, sg=sg):
                for sub in range(NSUB):
                    wait(s, s_pe, qkv_mark[20 + sub])
                    s_act.inc(s.activation(out=v_out[:, sub * 512:(sub + 1) * 512],
                                           in_=ps_proj[(20 + sub) % 3][:, 0:512],
                                           func=AF.Copy))

            last_cons = {}
            for o in range(NPROJ):
                b = o % 3
                if o < 20:
                    last_cons[b] = (s_dve, rope_marks[o][0])
                else:
                    last_cons[b] = (s_act, act_v_base + (o - 20) + 1)
            for b in range(3):
                bank_free[b] = last_cons[b]

            @block.scalar
            def _(s, sg=sg):
                wait(s, s_dve, dve_comb_last)
                wait(s, s_act, act_v_base + NSUB)
                for sub in range(NSUB):
                    tt = sg * NSUB + sub
                    s_sto.inc(s.dma_start(
                        out=QTp[tt].rearrange("p (c t) -> p c t", c=16),
                        in_=qt_out[:].rearrange("p (c t) -> p c t", c=16)
                        [:, :, sub * 128:(sub + 1) * 128]), 16)
                    s_sto.inc(s.dma_start(
                        out=KTp[tt].rearrange("p (c t) -> p c t", c=4),
                        in_=kt_out[:].rearrange("p (c t) -> p c t", c=4)
                        [:, :, sub * 128:(sub + 1) * 128]), 16)
                    s_sto.inc(s.dma_start(out=Vr[tt * 128:(tt + 1) * 128, :],
                                          in_=v_out[:, sub * 512:(sub + 1) * 512]), 16)
            st_watermark = s_sto.n

            @block.vector
            def _(v, sg=sg, stw=st_watermark):
                wait(v, s_sto, stw)
                s_dve.inc(v.engine_nop())

        pass_a_end = dict(pe=s_pe.n, dve=s_dve.n, act=s_act.n, pool=s_pool.n,
                          sto=s_sto.n)

        s_bar = Sem(ctx, nc, "barB")
        @block.vector
        def _(v):
            wait(v, s_pe)
            wait(v, s_pool)
            wait(v, s_act)
            wait(v, s_sto)
            wait(v, s_ldx)
            s_bar.inc(v.engine_nop())

        atx.close()
        patx.close()
        ctx_b = ExitStack()
        def sbb(name, shape, dt=F32):
            return ctx_b.enter_context(nc.sbuf_tensor(name, shape, dt))

        qt_t = [sbb(f"qt_t{i}", [128, 16 * 128], BF16) for i in range(2)]
        kt_t = [sbb(f"kt_t{i}", [128, 4 * 128], BF16) for i in range(2)]
        v_t = [sbb(f"v_t{i}", [128, 512], BF16) for i in range(2)]
        x_t = [sbb(f"x_t{i}", [128, D], F32) for i in range(2)]
        ktok = sbb("ktok", [128, 512], BF16)
        krs = sbb("krs", [128, 16 * 128], BF16)
        qd = sbb("qd", [128, 16 * 128], BF16)
        sc_bf = sbb("sc_bf", [128, 4 * 512], BF16)
        kv_sb = sbb("kv_sb", [128, H * 128], F32)
        kv_bf = sbb("kv_bf", [128, H * 128], BF16)
        oT = sbb("oT", [128, 16 * 128], BF16)
        x2_t2 = [sbb(f"x2_t{i}", [128, D], F32) for i in range(2)]
        sqs = sbb("sqs", [128, D], BF16)
        neg1 = sbb("neg1", [128, 1], F32)
        y_t = sbb("y_t", [128, D], BF16)
        yT_dbl = sbb("yT_dbl", [128, 16 * 256], BF16)
        m2 = sbb("m2", [128, 1], F32)
        v2 = sbb("v2", [128, 1], F32)
        r2 = sbb("r2", [128, 1], F32)
        mb2 = sbb("mb2", [128, 1], F32)
        sco = sbb("sco", [128, E], F32)
        top8 = sbb("top8", [128, 8], F32)
        top8i = sbb("top8i", [128, 8], U32)
        idxf = sbb("idxf", [128, TOPK], F32)
        wgt6 = sbb("wgt6", [128, TOPK], F32)
        wsum = sbb("wsum", [128, 1], F32)
        eqk = [sbb(f"eqk{k}", [128, E], F32) for k in range(TOPK)]
        Mbf = sbb("Mbf", [128, E], BF16)
        m01 = sbb("m01", [128, E], F32)
        m23 = sbb("m23", [128, E], F32)
        m45 = sbb("m45", [128, E], F32)
        posd = sbb("posd", [128, E], F32)
        pos6 = sbb("pos6", [128, TOPK], F32)
        ok6 = sbb("ok6", [128, TOPK], F32)
        slot6 = sbb("slot6", [128, TOPK], F32)
        slot6b = sbb("slot6b", [128, TOPK], F32)
        slot6u = sbb("slot6u", [128, TOPK], U32)
        tk6 = sbb("tk6", [128, E], F32)
        runoff = sbb("runoff", [1, E], F32)
        cnt_sb = sbb("cnt_sb", [1, E], F32)
        vals_all = sbb("vals_all", [128, 2 * TOPK], F32)
        tokid6 = sbb("tokid6", [128, TOPK], F32)
        t8v = sbb("t8v", [128, NT * 6], F32)
        slot_all = sbb("slot_all", [128, NT * 6], U32)
        s6 = sbb("s6", [128, 6], F32)

        wo_sb = sbb("wo_sb", [128, DCH * 2048], BF16)
        wg_sb = sbb("wg_sb", [128, DCH * E], BF16)

        ps_tr2 = [ctx_b.enter_context(nc.psum_tensor(f"ps_tr{i}", [128, 128], BF16))
                  for i in range(2)]
        ps_tr = ps_tr2[0]
        ps_sc = ctx_b.enter_context(nc.psum_tensor("ps_sc", [128, 512], F32))
        ps_o = ctx_b.enter_context(nc.psum_tensor("ps_o", [128, 512], F32))
        ps_kvu = ps_sc
        ps_at2 = [ctx_b.enter_context(nc.psum_tensor(f"ps_at{i}", [128, 512], F32))
                  for i in range(2)]
        ps_gp2 = ctx_b.enter_context(nc.psum_tensor("ps_gp2", [128, 2 * E], F32))
        ps_gt = ps_gp2[:, 0:E]
        ps_pos = ps_gp2[:, E:2 * E]
        ps_cc = ctx_b.enter_context(nc.psum_tensor("ps_cc", [128, E], F32))
        ps_cnt = ps_cc[0:1, :]
        ps_cs = ps_cc[32:33, :]

        bc_reg = [None]
        @block.gpsimd
        def _(g):
            wait(g, s_bar)
            for c in range(DCH):
                s_gp.inc(g.dma_start(out=wo_sb[:, c * 2048:(c + 1) * 2048],
                                     in_=woTp[c]), 16)
                s_gp.inc(g.dma_start(out=wg_sb[:, c * E:(c + 1) * E], in_=wgp[c]), 16)
            g.memset(kv_sb[:], 0.0)
            g.memset(kv_bf[:], 0.0)
            g.memset(runoff[:], 0.0)
            g.memset(neg1[:], -1.0)
            wait(g, s_ldt, 16)
            for k in range(TOPK):
                g.tensor_copy(out=tokid6[:, k:k + 1], in_=tokid_col[:])
            g.drain()
            s_pool.inc(g.engine_nop())
            bc_reg[0] = g.to_reg(E * ESTRIDE - 1)
        gp_wo_done = s_gp.n
        pool_b0 = s_pool.n

        s_ldb2 = [Sem(ctx, nc, "ldb0"), Sem(ctx, nc, "ldb1")]
        s_ldx2 = [Sem(ctx, nc, "ldbx0"), Sem(ctx, nc, "ldbx1")]
        s_sto2 = Sem(ctx, nc, "stoB")

        ktok_evacs = []
        scbf_marks = []
        oevac_marks = []
        kvupd_marks = []
        kvbf_marks = []
        x2_marks = []
        ytevac_marks = []
        scat_marks = []
        mk = {k: [] for k in ["ktok", "scbf", "oevac", "kvupd", "kvbf", "x2",
                              "posevac", "ytevac", "sig", "stores", "kvread",
                              "ldfree0", "ldfree1", "scat"]}
        kvbf_ready = None
        mk["yready"] = []
        mk["wgtsto"] = []
        mk["tkdone"] = []
        ld_marks = {}
        ld_marks_x = {}

        for it in range(NT + 1):
            p = it - 1
            t = it
            if it >= 1:
                x2p = x2_t2[p % 2]

            if it < NT:
                sl = t % 2
                x2b = x2_t2[t % 2]

                def emit_loads(tl):
                    slx = tl % 2
                    s_ldbx = s_ldb2[slx]
                    @block.sync
                    def _(sy, tl=tl, slx=slx, s_ldbx=s_ldbx):
                        if tl < 2:
                            wait(sy, s_bar)
                        else:
                            wait(sy, s_dve, mk["kvupd"][tl - 2])
                        s_ldbx.inc(sy.dma_start(out=qt_t[slx][:], in_=QTp[tl]), 16)
                        s_ldbx.inc(sy.dma_start(out=kt_t[slx][:], in_=KTp[tl]), 16)
                        s_ldbx.inc(sy.dma_start(out=v_t[slx][:],
                                                in_=Vr[tl * 128:(tl + 1) * 128, :]), 16)
                        ld_marks[tl] = s_ldbx.n
                        if tl >= 2:
                            wait(sy, s_dve, mk["x2"][tl - 2])
                        s_ldx2[slx].inc(sy.dma_start(out=x_t[slx][:],
                                                     in_=x_in[tl * 128:(tl + 1) * 128, :]), 16)
                        ld_marks_x[tl] = s_ldx2[slx].n
                if t == 0:
                    emit_loads(0)
                if t + 1 < NT:
                    emit_loads(t + 1)
                ld_mark = ld_marks[t]
                s_ldb = s_ldb2[sl]

                pe_sc = []
                dve_scbf_pred = s_dve.n
                @block.tensor
                def _(tt, t=t, sl=sl):
                    wait(tt, s_ldb, ld_mark)
                    for g4 in range(4):
                        if g4 > 0:
                            wait(tt, s_dve, dve_scbf_pred + g4)
                        elif t > 0:
                            wait(tt, s_dve, mk["scbf"][t - 1])
                            wait(tt, s_dve, mk["kvupd"][t - 1])
                        pe_sc.append(s_pe.inc(tt.matmul(
                            ps_sc[:], kt_t[sl][:, g4 * 128:(g4 + 1) * 128],
                            qt_t[sl][:, g4 * 512:(g4 + 1) * 512], start=True, stop=True)))

                @block.vector
                def _(v, t=t):
                    for g4 in range(4):
                        wait(v, s_pe, pe_sc[g4])
                        scbf_marks.append(s_dve.inc(v.tensor_mul(
                            out=sc_bf[:, g4 * 512:(g4 + 1) * 512], in0=ps_sc[:],
                            in1=decayT[:, g4 * 512:(g4 + 1) * 512])))
                assert s_dve.n == dve_scbf_pred + 4
                mk["scbf"].append(s_dve.n)

                pe_tr = []
                act_ktok_pred = s_act.n
                @block.tensor
                def _(tt, t=t, sl=sl):
                    if t == 0:
                        wait(tt, s_pool, pool_b0)
                    elif t == 1:
                        wait(tt, s_act, mk["ktok"][0])
                    else:
                        wait(tt, s_act, mk["ytevac"][t - 2])
                    for g4 in range(4):
                        if g4 > 1:
                            wait(tt, s_act, act_ktok_pred + g4 - 1)
                        pe_tr.append(s_pe.inc(tt.transpose(
                            ps_tr2[g4 % 2][:], kt_t[sl][:, g4 * 128:(g4 + 1) * 128], ident[:])))

                @block.scalar
                def _(s, t=t):
                    for g4 in range(4):
                        wait(s, s_pe, pe_tr[g4])
                        ktok_evacs.append(s_act.inc(s.activation(
                            out=ktok[:, g4 * 128:(g4 + 1) * 128], in_=ps_tr2[g4 % 2][:],
                            func=AF.Copy)))
                assert s_act.n == act_ktok_pred + 4
                mk["ktok"].append(s_act.n)

                dve_qd = []
                @block.vector
                def _(v, t=t, sl=sl):
                    wait(v, s_ldb, ld_mark)
                    for h in range(H):
                        dve_qd.append(s_dve.inc(v.tensor_mul(
                            out=qd[:, h * 128:(h + 1) * 128],
                            in0=qt_t[sl][:, h * 128:(h + 1) * 128],
                            in1=qdec_b[:, h * 128:(h + 1) * 128])))

                dve_krs = []
                @block.vector
                def _(v, t=t):
                    for h in range(H):
                        g4 = h // 4
                        wait(v, s_act, ktok_evacs[t * 4 + g4])
                        dve_krs.append(s_dve.inc(v.tensor_scalar(
                            out=krs[:, h * 128:(h + 1) * 128],
                            in0=ktok[:, g4 * 128:(g4 + 1) * 128],
                            scalar1=kdec_c[:, h:h + 1], scalar2=None, op0=ALU.mult)))

                pe_o = []
                act_oevac_pred = s_act.n
                @block.tensor
                def _(tt, t=t, sl=sl):
                    for g4 in range(4):
                        wait(tt, s_dve, scbf_marks[t * 4 + g4])
                        if g4 > 0:
                            wait(tt, s_act, act_oevac_pred + g4)
                        elif t > 0:
                            wait(tt, s_act, mk["oevac"][t - 1])
                        if t > 0 and g4 == 0:
                            wait(tt, s_act, kvbf_ready)
                        tt.matmul(ps_o[:], v_t[sl][:, g4 * 128:(g4 + 1) * 128],
                                  sc_bf[:, g4 * 512:(g4 + 1) * 512], start=True, stop=False)
                        mm = None
                        for hh in range(4):
                            h = g4 * 4 + hh
                            wait(tt, s_dve, dve_qd[h])
                            mm = tt.matmul(ps_o[:, hh * 128:(hh + 1) * 128],
                                           kv_bf[:, h * 128:(h + 1) * 128],
                                           qd[:, h * 128:(h + 1) * 128],
                                           start=False, stop=(hh == 3))
                        pe_o.append(s_pe.inc(mm))
                mk["kvread"].append(s_pe.n)

                @block.scalar
                def _(s, t=t):
                    for g4 in range(4):
                        wait(s, s_pe, pe_o[g4])
                        oevac_marks.append(s_act.inc(s.activation(
                            out=oT[:, g4 * 512:(g4 + 1) * 512], in_=ps_o[:], func=AF.Copy)))
                assert s_act.n == act_oevac_pred + 4
                mk["oevac"].append(s_act.n)

            if it >= 1:
                @block.vector
                def _(v, p=p, x2p=x2p):
                    v.drain()
                    v.reduce_sum(out=m2[:], in_=x2p[:], axis=AX.X)
                    v.drain()
                    v.tensor_scalar(out=m2[:], in0=m2[:], scalar1=1.0 / D, scalar2=None,
                                    op0=ALU.mult)
                    v.drain()
                    s_dve.inc(v.engine_nop())
                m2_done = s_dve.n

                @block.scalar
                def _(s, p=p, x2p=x2p, m2_done=m2_done):
                    wait(s, s_dve, m2_done)
                    s_act.inc(s.activation(out=sqs[:], in_=x2p[:], func=AF.Square,
                                           accum_out=v2[:]))
                sq_done = s_act.n

                @block.vector
                def _(v, p=p, sq_done=sq_done):
                    wait(v, s_act, sq_done)
                    v.tensor_mul(out=mb2[:], in0=m2[:], in1=m2[:])
                    v.drain()
                    v.scalar_tensor_tensor(out=v2[:], in0=v2[:], scalar=1.0 / D, in1=mb2[:],
                                           op0=ALU.mult, op1=ALU.subtract)
                    v.drain()
                    s_dve.inc(v.engine_nop())
                var_done = s_dve.n

                @block.scalar
                def _(s, p=p, var_done=var_done):
                    wait(s, s_dve, var_done)
                    s.activation(out=r2[:], in_=v2[:], func=AF.Ln, bias=eps_t[:, 0:1])
                    s.drain()
                    s.activation(out=r2[:], in_=r2[:], func=AF.Exp, scale=half_neg[:, 0:1])
                    s.drain()
                    s_act.inc(s.nop())
                r2_done = s_act.n

                @block.vector
                def _(v, p=p, r2_done=r2_done):
                    wait(v, s_act, r2_done)
                    v.tensor_scalar(out=mb2[:], in0=m2[:], scalar1=r2[:, 0:1], scalar2=-1.0,
                                    op0=ALU.mult, op1=ALU.mult)
                    v.drain()
                    s_dve.inc(v.engine_nop())
                mb2_done = s_dve.n

                @block.scalar
                def _(s, p=p, x2p=x2p, mb2_done=mb2_done):
                    wait(s, s_dve, mb2_done)
                    if p > 0:
                        wait(s, s_sto2, mk["stores"][p - 1])
                    s_act.inc(s.activation(out=y_t[:], in_=x2p[:], func=AF.Identity,
                                           bias=mb2[:, 0:1], scale=r2[:, 0:1]))
                y_ready = s_act.n
                mk["yready"].append(s_act.n)

            if it < NT:
                pe_kvu = []
                dve_kvu_pred = s_dve.n
                @block.tensor
                def _(tt, t=t, sl=sl):
                    wait(tt, s_dve, mk["scbf"][t])
                    for g4 in range(4):
                        if g4 > 0:
                            wait(tt, s_dve, dve_kvu_pred + 4 * g4)
                        elif t > 0:
                            wait(tt, s_dve, mk["kvupd"][t - 1])
                        mm = None
                        for hh in range(4):
                            h = g4 * 4 + hh
                            wait(tt, s_dve, dve_krs[h])
                            mm = tt.matmul(ps_kvu[:, hh * 128:(hh + 1) * 128],
                                           krs[:, h * 128:(h + 1) * 128],
                                           v_t[sl][:, g4 * 128:(g4 + 1) * 128],
                                           start=True, stop=True)
                        pe_kvu.append(s_pe.inc(mm))

                @block.vector
                def _(v, t=t):
                    for g4 in range(4):
                        wait(v, s_pe, pe_kvu[g4])
                        if t > 0:
                            wait(v, s_act, kvbf_marks[(t - 1) * 4 + g4])
                        for hh in range(4):
                            h = g4 * 4 + hh
                            m_ = s_dve.inc(v.scalar_tensor_tensor(
                                out=kv_sb[:, h * 128:(h + 1) * 128],
                                in0=kv_sb[:, h * 128:(h + 1) * 128],
                                scalar=float(math.exp(-128.0 * cfg.LAM[h])),
                                in1=ps_kvu[:, hh * 128:(hh + 1) * 128],
                                op0=ALU.mult, op1=ALU.add))
                        kvupd_marks.append(m_)
                assert s_dve.n == dve_kvu_pred + 16
                mk["kvupd"].append(s_dve.n)

            if it >= 1:
                pe_yt = []
                act_yte_pred = s_act.n
                @block.tensor
                def _(tt, p=p, y_ready=y_ready):
                    wait(tt, s_act, y_ready)
                    wait(tt, s_act, mk["ktok"][min(it, NT - 1)])
                    for c in range(16):
                        if c > 1:
                            wait(tt, s_act, act_yte_pred + c - 1)
                        pe_yt.append(s_pe.inc(tt.transpose(
                            ps_tr2[c % 2][:], y_t[:, c * 128:(c + 1) * 128], ident[:])))

                @block.scalar
                def _(s, p=p):
                    if p > 1 and p % 2 == 0:
                        wait(s, s_sto2, mk["stores"][p - 1])
                    for c in range(16):
                        wait(s, s_pe, pe_yt[c])
                        ytevac_marks.append(s_act.inc(s.activation(
                            out=yT_dbl[:, c * 256 + (p % 2) * 128:c * 256 + (p % 2) * 128 + 128],
                            in_=ps_tr2[c % 2][:], func=AF.Copy)))
                assert s_act.n == act_yte_pred + 16
                mk["ytevac"].append(s_act.n)

                pe_gt = []
                @block.tensor
                def _(tt, p=p):
                    if p > 0:
                        wait(tt, s_act, mk["sig"][p - 1])
                    else:
                        wait(tt, s_gp, gp_wo_done)
                    mm = None
                    for c in range(16):
                        wait(tt, s_act, ytevac_marks[p * 16 + c])
                        mm = tt.matmul(ps_gt,
                                       yT_dbl[:, c * 256 + (p % 2) * 128:c * 256 + (p % 2) * 128 + 128],
                                       wg_sb[:, c * E:(c + 1) * E], start=(c == 0), stop=(c == 15))
                    pe_gt.append(s_pe.inc(mm))

                @block.scalar
                def _(s, p=p, pe_gt=pe_gt):
                    wait(s, s_pe, pe_gt[0])
                    s_act.inc(s.activation(out=sco[:], in_=ps_gt, func=AF.Copy))
                mk["sig"].append(s_act.n)

            if it < NT:
                @block.scalar
                def _(s, t=t):
                    wait(s, s_pe, mk["kvread"][t])
                    for g4 in range(4):
                        wait(s, s_dve, kvupd_marks[t * 4 + g4])
                        m_ = None
                        for hh in range(4):
                            h = g4 * 4 + hh
                            m_ = s_act.inc(s.activation(out=kv_bf[:, h * 128:(h + 1) * 128],
                                                        in_=kv_sb[:, h * 128:(h + 1) * 128],
                                                        func=AF.Copy))
                        kvbf_marks.append(m_)
                kvbf_ready = s_act.n
                mk["kvbf"].append(s_act.n)

            if it >= 1:
                @block.vector
                def _(v, p=p):
                    wait(v, s_act, mk["sig"][p])
                    if p > 0:
                        wait(v, s_sto2, mk["stores"][p - 1])
                        wait(v, s_pool, mk["tkdone"][p - 1])
                    v.max(out=top8[:], in_=sco[:])
                    v.drain()
                    v.max_index(out=top8i[:], in_max=top8[:], in_values=sco[:])
                    v.drain()
                    v.tensor_copy(out=idxf[:], in_=top8i[:, 0:TOPK])
                    v.tensor_copy(out=t8v[:, p * 6:(p + 1) * 6], in_=top8[:, 0:TOPK])
                    v.drain()
                    for k in range(TOPK):
                        v.tensor_scalar(out=eqk[k][:], in0=iotaE[:], scalar1=idxf[:, k:k + 1],
                                        scalar2=None, op0=ALU.is_equal)
                    v.drain()
                    s_dve.inc(v.engine_nop())
                eq_done = s_dve.n

                @block.gpsimd
                def _(g, p=p, eq_done=eq_done):
                    wait(g, s_dve, eq_done)
                    g.tensor_add(out=m01[:], in0=eqk[0][:], in1=eqk[1][:])
                    g.tensor_add(out=m23[:], in0=eqk[2][:], in1=eqk[3][:])
                    g.tensor_add(out=m45[:], in0=eqk[4][:], in1=eqk[5][:])
                    g.drain()
                    g.tensor_add(out=m01[:], in0=m01[:], in1=m23[:])
                    g.drain()
                    g.tensor_add(out=Mbf[:], in0=m01[:], in1=m45[:])
                    g.drain()
                    s_pool.inc(g.engine_nop())
                M_done = s_pool.n

                @block.scalar
                def _(s, p=p, eq_done=eq_done):
                    wait(s, s_dve, eq_done)
                    s_act.inc(s.activation(out=s6[:], in_=t8v[:, p * 6:(p + 1) * 6],
                                           func=AF.Exp, scale=neg1[:, 0:1]))
                wexp_done = s_act.n

            if it < NT:
                pe_wo = []
                @block.tensor
                def _(tt, t=t):
                    if t == 0:
                        wait(tt, s_gp, gp_wo_done)
                    elif t > 0:
                        wait(tt, s_dve, x2_marks[(t - 1) * 4 + 2])
                    mm = None
                    for c in range(16):
                        wait(tt, s_act, oevac_marks[t * 4 + (c // 4)])
                        mm = tt.matmul(ps_at2[0][:],
                                       oT[:, c * 128:(c + 1) * 128],
                                       wo_sb[:, c * 2048 + 0 * 512:c * 2048 + 1 * 512],
                                       start=(c == 0), stop=(c == 15))
                    pe_wo.append(s_pe.inc(mm))

            if it >= 1:
                pe_pos = []
                @block.tensor
                def _(tt, p=p, M_done=M_done):
                    wait(tt, s_pool, M_done)
                    if p > 0:
                        wait(tt, s_dve, mk["posevac"][p - 1])
                    tt.matmul(ps_pos, LtriT[:], Mbf[:], start=True, stop=False)
                    tt.matmul(ps_pos, ones_row32[:], runoff[:], start=False, stop=True)
                    tt.matmul(ps_cs, ones_col[:], Mbf[:], start=True, stop=True)
                    pe_pos.append(s_pe.inc(tt.matmul(ps_cnt, ones_col[:], Mbf[:],
                                                     start=(p == 0), stop=(p == NT - 1))))

                @block.vector
                def _(v, p=p, pe_pos=pe_pos):
                    wait(v, s_pe, pe_pos[0])
                    v.tensor_copy(out=posd[:], in_=ps_pos)
                    v.tensor_add(out=runoff[:], in0=runoff[:], in1=ps_cs)
                    v.drain()
                    s_dve.inc(v.engine_nop())
                mk["posevac"].append(s_dve.n)
                pos_done = s_dve.n

                @block.gpsimd
                def _(g, p=p, pos_done=pos_done):
                    wait(g, s_dve, pos_done)
                    for k in range(TOPK):
                        g.tensor_mul(out=eqk[k][:], in0=posd[:], in1=eqk[k][:])
                    g.drain()
                    s_pool.inc(g.engine_nop())
                tk_done = s_pool.n
                mk["tkdone"].append(tk_done)

                @block.vector
                def _(v, p=p, tk_done=tk_done):
                    wait(v, s_pool, tk_done)
                    for k in range(TOPK):
                        v.reduce_sum(out=pos6[:, k:k + 1], in_=eqk[k][:], axis=AX.X)
                    v.drain()
                    v.tensor_scalar(out=ok6[:], in0=pos6[:], scalar1=float(CAP) - 0.5,
                                    scalar2=None, op0=ALU.is_lt)
                    v.scalar_tensor_tensor(out=slot6[:], in0=idxf[:], scalar=float(ESTRIDE),
                                           in1=pos6[:], op0=ALU.mult, op1=ALU.add)
                    v.drain()
                    v.tensor_scalar(out=slot6b[:], in0=slot6[:], scalar1=-100000.0,
                                    scalar2=None, op0=ALU.add)
                    v.drain()
                    v.tensor_mul(out=slot6b[:], in0=slot6b[:], in1=ok6[:])
                    v.drain()
                    v.tensor_scalar(out=slot6b[:], in0=slot6b[:], scalar1=100000.0,
                                    scalar2=None, op0=ALU.add)
                    v.drain()
                    v.tensor_copy(out=slot_all[:, p * 6:(p + 1) * 6], in_=slot6b[:])
                    v.drain()
                    s_dve.inc(v.engine_nop())
                slots_done = s_dve.n

                @block.vector
                def _(v, p=p, wexp_done=wexp_done):
                    wait(v, s_act, wexp_done)
                    if p == 0:
                        wait(v, s_pool, pool_b0)
                    else:
                        wait(v, s_scat, mk["scat"][-1])
                        wait(v, s_sto2, mk["wgtsto"][-1])
                    v.drain()
                    v.tensor_scalar(out=s6[:], in0=s6[:], scalar1=1.0, scalar2=None,
                                    op0=ALU.add)
                    v.drain()
                    v.reciprocal(out=s6[:], in_=s6[:])
                    v.drain()
                    v.reduce_sum(out=wsum[:], in_=s6[:], axis=AX.X)
                    v.drain()
                    v.reciprocal(out=wsum[:], in_=wsum[:])
                    v.drain()
                    v.tensor_scalar(out=wgt6[:], in0=s6[:], scalar1=wsum[:, 0:1],
                                    scalar2=None, op0=ALU.mult)
                    v.tensor_scalar(
                        out=vals_all[:].rearrange("p (k two) -> p k two", two=2)[:, :, 0:1],
                        in0=tokid6[:].rearrange("p (k o) -> p k o", o=1),
                        scalar1=float(p * 128), scalar2=None, op0=ALU.add)
                    v.drain()
                    v.tensor_copy(
                        out=vals_all[:].rearrange("p (k two) -> p k two", two=2)[:, :, 1:2],
                        in_=wgt6[:].rearrange("p (k o) -> p k o", o=1))
                    v.drain()
                    s_dve.inc(v.engine_nop())
                wgt_done = s_dve.n

            if it < NT:
                x2_pred = s_dve.n
                @block.tensor
                def _(tt, t=t, x2_pred=x2_pred):
                    for dg in range(1, 4):
                        if dg >= 2:
                            wait(tt, s_dve, x2_pred + (dg - 2) + 1)
                        elif t > 0:
                            wait(tt, s_dve, x2_marks[(t - 1) * 4 + 3])
                        mm = None
                        for c in range(16):
                            mm = tt.matmul(ps_at2[dg % 2][:],
                                           oT[:, c * 128:(c + 1) * 128],
                                           wo_sb[:, c * 2048 + dg * 512:c * 2048 + (dg + 1) * 512],
                                           start=(c == 0), stop=(c == 15))
                        pe_wo.append(s_pe.inc(mm))

                @block.vector
                def _(v, t=t, sl=sl, x2b=x2b):
                    wait(v, s_ldx2[sl], ld_marks_x[t])
                    if t >= 2:
                        wait(v, s_sto2, mk["stores"][t - 2])
                        wait(v, s_act, mk["yready"][t - 2])
                    for dg in range(4):
                        wait(v, s_pe, pe_wo[dg])
                        x2_marks.append(s_dve.inc(v.tensor_add(
                            out=x2b[:, dg * 512:(dg + 1) * 512], in0=ps_at2[dg % 2][:],
                            in1=x_t[sl][:, dg * 512:(dg + 1) * 512])))
                mk["x2"].append(s_dve.n)
                mk["ldfree0" if sl == 0 else "ldfree1"].append(s_dve.n)

            if it >= 1:
                gp_scat_pred = s_scat.n
                @block.gpsimd
                def _(g, p=p, wgt_done=wgt_done):
                    wait(g, s_dve, wgt_done)
                    for k in range(TOPK):
                        scat_marks.append(s_scat.inc(g.indirect_dma_start(
                            out=tab_out, out_offset=IOA(slot_all[:, p * 6 + k:p * 6 + k + 1], 0),
                            in_=vals_all[:, 2 * k:2 * k + 2], in_offset=None,
                            bounds_check=bc_reg[0], oob_is_err=False), 16))
                assert s_scat.n == gp_scat_pred + 16 * TOPK
                mk["scat"].append(s_scat.n)

                @block.gpsimd
                def _(g, p=p, x2p=x2p, y_ready=y_ready, slots_done=slots_done,
                      wgt_done=wgt_done):
                    wait(g, s_act, y_ready)
                    s_sto2.inc(g.dma_start(out=y_out[p * 128:(p + 1) * 128, :], in_=y_t[:]), 16)
                    wait(g, s_dve, mk["x2"][p])
                    s_sto2.inc(g.dma_start(out=x2_out[p * 128:(p + 1) * 128, :], in_=x2p[:]), 16)
                    wait(g, s_dve, slots_done)
                    s_sto2.inc(g.dma_start(out=idx_out[p * 128:(p + 1) * 128, :], in_=idxf[:]), 16)
                    s_sto2.inc(g.dma_start(out=pos_out[p * 128:(p + 1) * 128, :], in_=pos6[:]), 16)
                    wait(g, s_dve, wgt_done)
                    s_sto2.inc(g.dma_start(out=wgt_out[p * 128:(p + 1) * 128, :], in_=wgt6[:]), 16)
                    if p % 2 == 1:
                        wait(g, s_act, mk["ytevac"][p])
                        s_sto2.inc(g.dma_start(out=yTp[p // 2], in_=yT_dbl[:]), 16)
                mk["stores"].append(s_sto2.n)
                mk["wgtsto"].append(s_sto2.n)

        @block.vector
        def _(v):
            wait(v, s_pe)
            v.tensor_copy(out=cnt_sb[:], in_=ps_cnt)
            v.drain()
            s_dve.inc(v.engine_nop())

        @block.gpsimd
        def _(g):
            wait(g, s_dve)
            s_sto2.inc(g.dma_start(out=cnt_out, in_=cnt_sb[:]), 16)

        @block.sync
        def _(sy):
            wait(sy, s_sto)
            wait(sy, s_sto2)
            wait(sy, s_gp)
            wait(sy, s_scat)

        ctx_b.close()

    ctx.close()
    return nc



def pack_l2_inputs(cfg, core_id, y_all, tabs_all, cnt_all, w_up, w_down):
    e0 = core_id * 8
    wup_t = np.ascontiguousarray(w_up[e0:e0 + 8].reshape(8, DCH, 128, 4, 128))
    wdn_t = np.ascontiguousarray(w_down[e0:e0 + 8].reshape(8, 2, 128, 2048))
    e0col = np.full((128, 1), float(e0), np.float32)
    return {"y_all": y_all, "tabs": tabs_all, "cnt_all": cnt_all,
            "wup": wup_t, "wdn": wdn_t, "e0col": e0col}


def build_l2(cfg, ncore=8):
    NG = ncore * cfg.S
    CAP = cfg.CAP
    BT = F32 if cfg.fp32 else BF16
    nc = bass.Bass("TRN2", target_bir_lowering=False)

    y_all = nc.dram_tensor("y_all", [NG, D], BT, kind="ExternalInput").ap()
    tabs = nc.dram_tensor("tabs", [ncore * E * ESTRIDE, 2], F32, kind="ExternalInput").ap()
    cnt_in = nc.dram_tensor("cnt_all", [ncore, E], F32, kind="ExternalInput").ap()
    wup_in = nc.dram_tensor("wup", [8, DCH, 128, 4, 128], F32, kind="ExternalInput").ap()
    wdn_in = nc.dram_tensor("wdn", [8, 2, 128, 2048], F32, kind="ExternalInput").ap()
    e0_in = nc.dram_tensor("e0col", [128, 1], F32, kind="ExternalInput").ap()
    y2_out = nc.dram_tensor("y2", [8 * ESTRIDE, D], BT, kind="ExternalOutput").ap()

    NST = ESTRIDE // 128
    ctx = ExitStack()
    def sbt(name, shape, dt=F32):
        return ctx.enter_context(nc.sbuf_tensor(name, shape, dt))
    sems = {}
    def S_(name):
        sems[name] = Sem(ctx, nc, name)
        return sems[name]
    s_pe = S_("pe"); s_dve = S_("dve"); s_act = S_("act"); s_pool = S_("pool")
    s_gpw = S_("gpw"); s_gt = S_("gt"); s_ld = S_("ld"); s_sto = S_("sto")
    s_gy = [S_("gy0"), S_("gy1")]

    ident = sbt("ident", [128, 128], BT)
    ident32 = sbt("ident32", [8, 8], F32)
    ltri8 = sbt("ltri8", [8, 8], F32)
    grid128 = sbt("grid128", [128, 128], F32)
    grid8 = sbt("grid8", [8, 8], F32)
    pcol8 = sbt("pcol8", [8, 128], F32)
    cnt_sb = sbt("cnt_sb", [8, E], F32)
    pref_sb = sbt("pref_sb", [8, E], F32)
    ohT = sbt("ohT", [E, 8], F32)
    prefT_sb = sbt("prefT_sb", [E, 8], F32)
    cntT_sb = sbt("cntT_sb", [E, 8], F32)
    myp = sbt("myp", [8, 8], F32)
    myc = sbt("myc", [8, 8], F32)
    sel8 = [sbt(f"sel8_{i}", [8, 128], F32) for i in range(8)]
    pref_b = sbt("pref_b", [128, 8 * 8], F32)
    cnt_b = sbt("cnt_b", [128, 8 * 8], F32)
    joff = sbt("joff", [128, 8], F32)
    P64 = sbt("P64", [128, 64], F32)
    dj64 = sbt("dj64", [128, 64], F32)
    selm64 = sbt("selm64", [128, 64], F32)
    sel264 = sbt("sel264", [128, 64], F32)
    tt64 = sbt("tt64", [128, 64], F32)
    rsA = sbt("rsA", [128, 64], F32)
    rsB = sbt("rsB", [128, 64], F32)
    rsC = sbt("rsC", [128, 64], F32)
    srcu64 = sbt("srcu64", [128, 64], U32)
    tokw64 = sbt("tokw64", [128, 128], F32)
    toku64 = sbt("toku64", [128, 64], U32)
    wcol64 = sbt("wcol64", [128, 64], F32)
    iotap = sbt("iotap", [128, 8], F32)
    e0sb = sbt("e0sb", [128, 1], F32)
    abase = sbt("abase", [128, 8], F32)
    ptile = sbt("ptile", [128, 8], F32)
    dj = sbt("dj", [128, 8], F32)
    selm = sbt("selm", [128, 8], F32)
    sel2 = sbt("sel2", [128, 8], F32)
    tt8 = sbt("tt8", [128, 8], F32)
    rs3 = sbt("rs3", [128, 3], F32)
    srcf = sbt("srcf", [128, 1], F32)
    srcu = sbt("srcu", [128, 1], U32)
    tokw = sbt("tokw", [128, 2], F32)
    toku = sbt("toku", [128, 1], U32)
    wcol = sbt("wcol", [128, NST], F32)
    ybuf = [sbt(f"ybuf{i}", [128, D], BT) for i in range(2)]
    xeT = [sbt(f"xeT{i}", [128, 16 * 512], BT) for i in range(2)]
    wup_sb = [sbt(f"wup_sb{i}", [128, 16 * 512], BT) for i in range(2)]
    wdn_sb = [sbt(f"wdn_sb{i}", [128, 2 * 2048], BT) for i in range(2)]
    sg_sb2 = sbt("sg_sb2", [128, 1024], F32)
    h_bf = sbt("h_bf", [128, 2 * 512], BT)
    y2_t = sbt("y2_t", [128, D], BT)

    ps_tr = ctx.enter_context(nc.psum_tensor("ps_tr", [128, 128], BT))
    ps_up = [ctx.enter_context(nc.psum_tensor(f"ps_up{i}", [128, 512], F32))
             for i in range(4)]
    ps_dn = [ctx.enter_context(nc.psum_tensor(f"ps_dn{i}", [128, 512], F32))
             for i in range(2)]
    pfctx = ExitStack()
    ps_pf = pfctx.enter_context(nc.psum_tensor("ps_pf", [128, 64], F32))

    with nc.Block() as block:
        @block.gpsimd
        def _(g):
            g.iota(out=grid128[:], pattern=[[-1, 128]], base=0, channel_multiplier=1,
                   allow_small_or_imprecise_dtypes=True)
            g.iota(out=grid8[:], pattern=[[-1, 8]], base=0, channel_multiplier=1,
                   allow_small_or_imprecise_dtypes=True)
            g.iota(out=pcol8[:], pattern=[[0, 128]], base=0, channel_multiplier=1,
                   allow_small_or_imprecise_dtypes=True)
            g.iota(out=ohT[:], pattern=[[-1, 8]], base=0, channel_multiplier=1,
                   allow_small_or_imprecise_dtypes=True)
            g.iota(out=joff[:], pattern=[[1, 8]], base=0, channel_multiplier=0,
                   allow_small_or_imprecise_dtypes=True)
            g.iota(out=iotap[:], pattern=[[0, 8]], base=0, channel_multiplier=1,
                   allow_small_or_imprecise_dtypes=True)
            g.iota(out=P64[:], pattern=[[128, 8], [0, 8]], base=0, channel_multiplier=1,
                   allow_small_or_imprecise_dtypes=True)
            g.drain()
            s_pool.inc(g.engine_nop())

        @block.vector
        def _(v):
            wait(v, s_pool)
            v.tensor_scalar(out=joff[:], in0=joff[:], scalar1=float(E * ESTRIDE),
                            scalar2=None, op0=ALU.mult)
            v.tensor_scalar(out=grid128[:], in0=grid128[:], scalar1=0.0, scalar2=None,
                            op0=ALU.is_equal)
            v.tensor_scalar(out=ident32[:], in0=grid8[0:8, :], scalar1=0.0, scalar2=None,
                            op0=ALU.is_equal)
            v.tensor_scalar(out=ltri8[:], in0=grid8[0:8, :], scalar1=-0.5, scalar2=None,
                            op0=ALU.is_lt)
            for le in range(8):
                v.tensor_scalar(out=sel8[le][:], in0=pcol8[:], scalar1=float(le),
                                scalar2=None, op0=ALU.is_equal)
            v.drain()
            v.tensor_copy(out=ident[:], in_=grid128[:])
            v.drain()
            s_dve.inc(v.engine_nop())

        @block.gpsimd
        def _(g):
            wait(g, s_dve)
            s_pool.inc(g.engine_nop())

        @block.sync
        def _(sy):
            s_ld.inc(sy.dma_start(out=cnt_sb[:], in_=cnt_in), 16)
            s_ld.inc(sy.dma_start(out=e0sb[:], in_=e0_in), 16)

        @block.vector
        def _(v):
            wait(v, s_pool)
            wait(v, s_ld)
            v.tensor_scalar(out=ohT[:], in0=ohT[:], scalar1=e0sb[0:E, 0:1],
                            scalar2=None, op0=ALU.is_equal)
            for le in range(8):
                v.tensor_scalar(out=abase[:, le:le + 1], in0=e0sb[:],
                                scalar1=float(ESTRIDE), scalar2=float(ESTRIDE * le),
                                op0=ALU.mult, op1=ALU.add)
            v.drain()
            s_dve.inc(v.engine_nop())
        dve_oh = s_dve.n

        @block.tensor
        def _(t):
            wait(t, s_dve, dve_oh)
            s_pe.inc(t.matmul(ps_pf[0:8, 0:E], ltri8[:], cnt_sb[:], start=True, stop=True))
        pe_pref = s_pe.n

        @block.vector
        def _(v):
            wait(v, s_pe, pe_pref)
            v.tensor_copy(out=pref_sb[:], in_=ps_pf[0:8, 0:E])
            v.drain()
            s_dve.inc(v.engine_nop())
        dve_prefsb = s_dve.n

        @block.tensor
        def _(t):
            wait(t, s_dve, dve_prefsb)
            t.transpose(ps_up[0][0:E, 0:8], cnt_sb[:], ident32[:])
            s_pe.inc(t.transpose(ps_up[1][0:E, 0:8], pref_sb[:], ident32[:]))
        pe_tr2 = s_pe.n

        @block.vector
        def _(v):
            wait(v, s_pe, pe_tr2)
            v.tensor_copy(out=cntT_sb[:], in_=ps_up[0][0:E, 0:8])
            v.tensor_copy(out=prefT_sb[:], in_=ps_up[1][0:E, 0:8])
            v.drain()
            s_dve.inc(v.engine_nop())
        dve_trsb = s_dve.n

        @block.tensor
        def _(t):
            wait(t, s_dve, dve_trsb)
            t.matmul(ps_pf[0:8, 0:8], ohT[:], prefT_sb[:], start=True, stop=True)
            s_pe.inc(t.matmul(ps_pf[32:40, 0:8], ohT[:], cntT_sb[:], start=True, stop=True))
        pe_my = s_pe.n

        @block.vector
        def _(v):
            wait(v, s_pe, pe_my)
            v.tensor_copy(out=myp[:], in_=ps_pf[0:8, 0:8])
            v.tensor_copy(out=myc[:], in_=ps_pf[32:40, 0:8])
            v.drain()
            s_dve.inc(v.engine_nop())
        dve_my = s_dve.n

        dve_ble_pred = s_dve.n
        @block.tensor
        def _(t):
            wait(t, s_dve, dve_my)
            for le in range(8):
                if le > 0:
                    wait(t, s_dve, dve_ble_pred + le)
                t.matmul(ps_pf[:, 0:8], sel8[le][:], myp[:], start=True, stop=True)
                s_pe.inc(t.matmul(ps_pf[:, 8:16], sel8[le][:], myc[:], start=True, stop=True))
        pe_ble = s_pe.n - 8

        @block.vector
        def _(v):
            for le in range(8):
                wait(v, s_pe, pe_ble + le + 1)
                v.tensor_copy(out=pref_b[:, le * 8:(le + 1) * 8], in_=ps_pf[:, 0:8])
                s_dve.inc(v.tensor_copy(out=cnt_b[:, le * 8:(le + 1) * 8],
                                        in_=ps_pf[:, 8:16]))
        assert s_dve.n == dve_ble_pred + 8
        dve_b = s_dve.n

        w_marks = []
        @block.gpsimd
        def _(g):
            for le in range(2):
                for c in range(DCH):
                    s_gpw.inc(g.dma_start(
                        out=wup_sb[le][:, c * 512:(c + 1) * 512],
                        in_=wup_in[le, c].rearrange("p a b -> p (a b)")), 16)
                for hc in range(2):
                    s_gpw.inc(g.dma_start(
                        out=wdn_sb[le][:, hc * 2048:(hc + 1) * 2048],
                        in_=wdn_in[le, hc]), 16)
                w_marks.append(s_gpw.n)
                g.wait_ge(s_gpw.h, s_gpw.n)

        pfctx.close()
        ps_trB = ctx.enter_context(nc.psum_tensor("ps_trB", [128, 128], BT))
        ps_tr2 = [ps_tr, ps_trB]
        mkk = {"xeT": [[], []], "ybuffree": [[], []],
               "updone": [], "dndone": [], "siludone": [],
               "g2": [], "tok": []}
        first_dn = [True]

        @block.vector
        def _(v):
            wait(v, s_dve, dve_b)
            for le in range(8):
                pj = P64
                pb = pref_b[:, le * 8:(le + 1) * 8].rearrange(
                    "p (o j) -> p o j", o=1).broadcast_to([128, 8, 8])
                cb = cnt_b[:, le * 8:(le + 1) * 8].rearrange(
                    "p (o j) -> p o j", o=1).broadcast_to([128, 8, 8])
                dj3 = dj64[:].rearrange("p (s j) -> p s j", s=8)
                sm3 = selm64[:].rearrange("p (s j) -> p s j", s=8)
                s23 = sel264[:].rearrange("p (s j) -> p s j", s=8)
                v.tensor_sub(out=dj3, in0=P64[:].rearrange("p (s j) -> p s j", s=8), in1=pb)
                v.drain()
                v.tensor_scalar(out=selm64[:], in0=dj64[:], scalar1=-0.5, scalar2=None,
                                op0=ALU.is_gt)
                v.tensor_tensor(out=s23, in0=dj3, in1=cb, op=ALU.is_lt)
                v.drain()
                v.tensor_mul(out=selm64[:], in0=selm64[:], in1=sel264[:])
                v.drain()
                v.reduce_sum(out=rsA[:, le * 8:(le + 1) * 8]
                             .rearrange("p (s j) -> p s j", s=8, j=1), in_=sm3, axis=AX.X)
                v.tensor_mul(out=tt64[:], in0=selm64[:], in1=dj64[:])
                v.drain()
                v.reduce_sum(out=rsB[:, le * 8:(le + 1) * 8]
                             .rearrange("p (s j) -> p s j", s=8, j=1),
                             in_=tt64[:].rearrange("p (s j) -> p s j", s=8), axis=AX.X)
                v.drain()
                v.tensor_mul(out=tt64[:], in0=selm64[:],
                             in1=joff[:].rearrange("p (o j) -> p o j", o=1)
                             .broadcast_to([128, 8, 8]))
                v.drain()
                v.reduce_sum(out=rsC[:, le * 8:(le + 1) * 8]
                             .rearrange("p (s j) -> p s j", s=8, j=1),
                             in_=tt64[:].rearrange("p (s j) -> p s j", s=8), axis=AX.X)
                v.drain()
                v.tensor_scalar(out=rsA[:, le * 8:(le + 1) * 8],
                                in0=rsA[:, le * 8:(le + 1) * 8],
                                scalar1=abase[:, le:le + 1], scalar2=None, op0=ALU.mult)
                v.drain()
            v.tensor_add(out=rsA[:], in0=rsA[:], in1=rsB[:])
            v.drain()
            v.tensor_add(out=rsA[:], in0=rsA[:], in1=rsC[:])
            v.drain()
            v.tensor_copy(out=srcu64[:], in_=rsA[:])
            v.drain()
            s_dve.inc(v.engine_nop())
        src_all_done = s_dve.n

        @block.gpsimd
        def _(g):
            wait(g, s_dve, src_all_done)
            for i in range(64):
                s_gt.inc(g.indirect_dma_start(
                    out=tokw64[:, i * 2:(i + 1) * 2], out_offset=None, in_=tabs,
                    in_offset=IOA(srcu64[:, i:i + 1], 0)), 16)
        gt_all = s_gt.n

        @block.vector
        def _(v):
            wait(v, s_gt, gt_all)
            t3 = tokw64[:].rearrange("p (i b) -> p i b", b=2)
            v.tensor_copy(out=toku64[:], in_=t3[:, :, 0])
            v.tensor_copy(out=wcol64[:], in_=t3[:, :, 1])
            v.drain()
            s_dve.inc(v.engine_nop())
        tok_all_done = s_dve.n

        h_done_prev = [None]
        dn_read_hbf = [None]
        t_sto_war = [None]
        dn_pe_last = [None]
        wdn_pending = []
        for le in range(8):
            wsl = le % 2
            for st in range(NST):
                gi = le * NST + st
                @block.gpsimd
                def _(g, le=le, st=st, gi=gi):
                    if le == 0 and st == 0:
                        wait(g, s_dve, tok_all_done)
                    if len(mkk["ybuffree"][st % 2]) >= 1:
                        wait(g, s_pe, mkk["ybuffree"][st % 2][-1])
                    s_gy[st % 2].inc(g.indirect_dma_start(
                        out=ybuf[st % 2][:], out_offset=None, in_=y_all,
                        in_offset=IOA(toku64[:, gi:gi + 1], 0)), 16)
                g2 = s_gy[st % 2].n
                mkk["g2"].append(g2)

                pe_trm = []
                act_tr0 = s_act.n
                @block.tensor
                def _(t, le=le, st=st):
                    wait(t, s_gy[st % 2], g2)
                    sgx = st // 4
                    if le > 0:
                        wait(t, s_pe, mkk["updone"][(le - 1) * 2 + sgx])
                    for c in range(16):
                        if c > 1:
                            wait(t, s_act, act_tr0 + c - 1)
                        elif c == 0:
                            wait(t, s_act, act_tr0)
                        pe_trm.append(s_pe.inc(t.transpose(
                            ps_tr2[c % 2][:], ybuf[st % 2][:, c * 128:(c + 1) * 128],
                            ident[:])))
                mkk["ybuffree"][st % 2].append(s_pe.n)

                @block.scalar
                def _(s, le=le, st=st):
                    for c in range(16):
                        wait(s, s_pe, pe_trm[c])
                        s_act.inc(s.activation(
                            out=xeT[st // 4][:, c * 512 + (st % 4) * 128:
                                             c * 512 + (st % 4) * 128 + 128],
                            in_=ps_tr2[c % 2][:], func=AF.Copy))
                assert s_act.n == act_tr0 + 16
                if st % 4 == 3:
                    mkk["xeT"][st // 4].append(s_act.n)

            if wdn_pending:
                lep = wdn_pending.pop(0)
                @block.gpsimd
                def _(g, lep=lep):
                    wait(g, s_pe, dn_pe_last[0])
                    for hc in range(2):
                        s_gpw.inc(g.dma_start(
                            out=wdn_sb[lep % 2][:, hc * 2048:(hc + 1) * 2048],
                            in_=wdn_in[lep + 2, hc]), 16)
                w_marks.append(s_gpw.n)

            for sg2 in range(2):
                pe_upm = []
                @block.tensor
                def _(t, le=le, sg2=sg2):
                    wait(t, s_act, mkk["xeT"][sg2][-1])
                    wait(t, s_gpw, w_marks[le])
                    if mkk["siludone"]:
                        wait(t, s_act, mkk["siludone"][-1])
                    if mkk["updone"]:
                        wait(t, s_dve, h_done_prev[0])
                    for hp in range(4):
                        mm = None
                        for c in range(16):
                            mm = t.matmul(ps_up[hp][:],
                                          wup_sb[le % 2][:, (c * 4 + hp) * 128:
                                                         (c * 4 + hp + 1) * 128],
                                          xeT[sg2][:, c * 512:(c + 1) * 512],
                                          start=(c == 0), stop=(c == 15))
                        pe_upm.append(s_pe.inc(mm))
                mkk["updone"].append(s_pe.n)

                act_silu0 = s_act.n
                @block.scalar
                def _(s, le=le, sg2=sg2):
                    if h_done_prev[0] is not None:
                        wait(s, s_dve, h_done_prev[0])
                    for hc in range(2):
                        wait(s, s_pe, pe_upm[hc])
                        s_act.inc(s.activation(out=sg_sb2[:, hc * 512:(hc + 1) * 512],
                                               in_=ps_up[hc][:], func=AF.Silu))
                mkk["siludone"].append(s_act.n)

                @block.vector
                def _(v, le=le, sg2=sg2):
                    if dn_read_hbf[0] is not None:
                        wait(v, s_pe, dn_read_hbf[0])
                    for hc in range(2):
                        wait(v, s_act, act_silu0 + hc + 1)
                        wait(v, s_pe, pe_upm[2 + hc])
                        s_dve.inc(v.tensor_mul(out=h_bf[:, hc * 512:(hc + 1) * 512],
                                               in0=sg_sb2[:, hc * 512:(hc + 1) * 512],
                                               in1=ps_up[2 + hc][:]))
                h_done = s_dve.n
                h_done_prev[0] = h_done

                for st2 in range(4):
                    stg = sg2 * 4 + st2
                    pe_dnm = []
                    act_ev0 = s_act.n
                    @block.tensor
                    def _(t, le=le, sg2=sg2, st2=st2):
                        wait(t, s_dve, h_done)
                        for dg in range(4):
                            if not first_dn[0] or dg >= 2:
                                wait(t, s_act, act_ev0 + dg - 1)
                            mm = None
                            for hc in range(2):
                                mm = t.matmul(
                                    ps_dn[dg % 2][:],
                                    h_bf[:, hc * 512 + st2 * 128:hc * 512 + st2 * 128 + 128],
                                    wdn_sb[le % 2][:, hc * 2048 + dg * 512:
                                                   hc * 2048 + (dg + 1) * 512],
                                    start=(hc == 0), stop=(hc == 1))
                            pe_dnm.append(s_pe.inc(mm))
                        first_dn[0] = False
                    dn_last_pe = s_pe.n

                    @block.scalar
                    def _(s, le=le, sg2=sg2, st2=st2, stg=stg):
                        if t_sto_war[0] is not None:
                            wait(s, s_sto, t_sto_war[0])
                        for dg in range(4):
                            wait(s, s_pe, pe_dnm[dg])
                            s_act.inc(s.activation(
                                out=y2_t[:, dg * 512:(dg + 1) * 512], in_=ps_dn[dg % 2][:],
                                func=AF.Identity,
                                scale=wcol64[:, le * NST + stg:le * NST + stg + 1],
                                bias=0.0))
                        wait(s, s_act)
                        s_sto.inc(s.dma_start(
                            out=y2_out[(le * NST + stg) * 128:(le * NST + stg + 1) * 128, :],
                            in_=y2_t[:]), 16)
                    assert s_act.n == act_ev0 + 4
                    mkk["dndone"].append(s_act.n)
                    t_sto_war[0] = s_sto.n
                dn_pe_last[0] = s_pe.n
                dn_read_hbf[0] = s_pe.n

            if le + 2 < 8:
                @block.gpsimd
                def _(g, le=le):
                    wait(g, s_gpw, w_marks[-1])
                    wait(g, s_pe, mkk["updone"][le * 2 + 1])
                    for c in range(DCH):
                        s_gpw.inc(g.dma_start(
                            out=wup_sb[le % 2][:, c * 512:(c + 1) * 512],
                            in_=wup_in[le + 2, c].rearrange("p a b -> p (a b)")), 16)
                wdn_pending.append(le)

        @block.sync
        def _(sy):
            wait(sy, s_sto)
            wait(sy, s_gt)
            wait(sy, s_gy[0])
            wait(sy, s_gy[1])

    ctx.close()
    return nc



def pack_l3_inputs(cfg, core_id, ncore, x2_c, yTp_c, idx_c, pos_c, cnt_all, y2_all,
                   w_su, w_sd):
    wsu_t = np.ascontiguousarray(w_su.reshape(DCH, 128, 1024))
    wsd_t = np.ascontiguousarray(w_sd.reshape(4, 128, 2048))
    pm = (np.arange(ncore) < core_id).astype(np.float32).reshape(ncore, 1)
    nz = y2_all.shape[0]
    y2z = np.concatenate([y2_all, np.zeros((128, D), y2_all.dtype)], 0)
    return {"x2": x2_c, "yTp": yTp_c, "idxo": idx_c, "poso": pos_c,
            "cnt_all": cnt_all, "y2z": y2z, "wsu": wsu_t, "wsd": wsd_t,
            "prevm": pm}


def build_l3(cfg, ncore=8):
    S, CAP = cfg.S, cfg.CAP
    BT = F32 if cfg.fp32 else BF16
    NZ = ncore * 8 * ESTRIDE
    NS3 = S // 256
    nc = bass.Bass("TRN2", target_bir_lowering=False)

    x2_in = nc.dram_tensor("x2", [S, D], F32, kind="ExternalInput").ap()
    yTp_in = nc.dram_tensor("yTp", [NS3, 128, 16 * 256], BT, kind="ExternalInput").ap()
    idx_in = nc.dram_tensor("idxo", [S, TOPK], F32, kind="ExternalInput").ap()
    pos_in = nc.dram_tensor("poso", [S, TOPK], F32, kind="ExternalInput").ap()
    cnt_in = nc.dram_tensor("cnt_all", [ncore, E], F32, kind="ExternalInput").ap()
    y2z_in = nc.dram_tensor("y2z", [NZ + 128, D], BT, kind="ExternalInput").ap()
    wsu_in = nc.dram_tensor("wsu", [DCH, 128, 1024], F32, kind="ExternalInput").ap()
    wsd_in = nc.dram_tensor("wsd", [4, 128, 2048], F32, kind="ExternalInput").ap()
    pm_in = nc.dram_tensor("prevm", [ncore, 1], F32, kind="ExternalInput").ap()

    out_o = nc.dram_tensor("out", [S, D], F32, kind="ExternalOutput").ap()

    ctx = ExitStack()
    def sbt(name, shape, dt=F32):
        return ctx.enter_context(nc.sbuf_tensor(name, shape, dt))
    sems = {}
    def S_(name):
        sems[name] = Sem(ctx, nc, name)
        return sems[name]
    s_pe = S_("pe"); s_dve = S_("dve"); s_act = S_("act"); s_pool = S_("pool")
    s_gpw = S_("gpw"); s_ld = S_("ld")
    s_sto2 = [S_("sto0"), S_("sto1")]
    s_gy = [S_("gy0"), S_("gy1"), S_("gy2")]
    s_ldx = [S_("ldx0"), S_("ldx1")]

    iotaE = sbt("iotaE", [128, E], F32)
    ones_row32 = sbt("ones_row32", [1, 128], F32)
    cnt_sb = sbt("cnt_sb", [ncore, E], F32)
    pm_sb = sbt("pm_sb", [ncore, 1], F32)
    prior_r = sbt("prior_r", [1, E], F32)
    prior_b = sbt("prior_b", [128, E], F32)
    wsu_sb = sbt("wsu_sb", [128, DCH * 1024], BT)
    wsd_sb = sbt("wsd_sb", [128, 4 * 2048], BT)
    yT_t = [sbt(f"yT_t{i}", [128, 16 * 256], BT) for i in range(2)]
    x2_t = [sbt(f"x2_t{i}", [128, 2 * 2048], F32) for i in range(2)]
    idx_t = [sbt(f"idx_t{i}", [128, 2 * TOPK], F32) for i in range(2)]
    pos_t = [sbt(f"pos_t{i}", [128, 2 * TOPK], F32) for i in range(2)]
    sgs = sbt("sgs", [128, 4 * 256], F32)
    h_sb = sbt("h_sb", [128, 4 * 256], BT)
    acc = [sbt(f"acc{i}", [128, D], F32) for i in range(2)]
    gbuf2 = [[sbt(f"gbuf{j}_{i}", [128, D], BT) for i in range(6)] for j in range(2)]
    eq6 = sbt("eq6", [128, TOPK * E], F32)
    pk6 = sbt("pk6", [128, TOPK], F32)
    gidx = sbt("gidx", [128, TOPK], F32)
    gidxb = sbt("gidxb", [128, TOPK], F32)
    okc = sbt("okc", [128, TOPK], F32)
    gidxu2 = [sbt(f"gidxu{i}", [128, TOPK], U32) for i in range(2)]

    ps_su4 = [ctx.enter_context(nc.psum_tensor(f"ps_su{i}", [128, 512], F32))
              for i in range(4)]
    ps_su = [ps_su4[i // 2][:, (i % 2) * 256:(i % 2 + 1) * 256] for i in range(8)]
    ps_dn = [ctx.enter_context(nc.psum_tensor(f"ps_dn{i}", [128, 512], F32))
             for i in range(2)]
    ps_pr = ctx.enter_context(nc.psum_tensor("ps_pr", [128, E], F32))

    with nc.Block() as block:
        @block.gpsimd
        def _(g):
            g.memset(ones_row32[:], 1.0)
            g.iota(out=iotaE[:], pattern=[[1, E]], base=0, channel_multiplier=0,
                   allow_small_or_imprecise_dtypes=True)
            g.drain()
            s_pool.inc(g.engine_nop())
            for c in range(DCH):
                s_gpw.inc(g.dma_start(out=wsu_sb[:, c * 1024:(c + 1) * 1024],
                                      in_=wsu_in[c]), 16)
            for hc in range(4):
                s_gpw.inc(g.dma_start(out=wsd_sb[:, hc * 2048:(hc + 1) * 2048],
                                      in_=wsd_in[hc]), 16)
        gpw_done = s_gpw.n

        @block.sync
        def _(sy):
            s_ld.inc(sy.dma_start(out=cnt_sb[:], in_=cnt_in), 16)
            s_ld.inc(sy.dma_start(out=pm_sb[:], in_=pm_in), 16)
        ld0 = s_ld.n

        @block.tensor
        def _(t):
            wait(t, s_ld, ld0)
            wait(t, s_pool)
            s_pe.inc(t.matmul(ps_pr[0:1, :], pm_sb[:], cnt_sb[:], start=True, stop=True))
        pe_pr = s_pe.n

        @block.vector
        def _(v):
            wait(v, s_pe, pe_pr)
            v.tensor_copy(out=prior_r[:], in_=ps_pr[0:1, :])
            v.drain()
            s_dve.inc(v.engine_nop())
        dve_pr = s_dve.n

        @block.tensor
        def _(t):
            wait(t, s_dve, dve_pr)
            s_pe.inc(t.matmul(ps_pr[:], ones_row32[:], prior_r[:], start=True, stop=True))
        pe_pb = s_pe.n

        @block.vector
        def _(v):
            wait(v, s_pe, pe_pb)
            v.tensor_copy(out=prior_b[:], in_=ps_pr[:])
            v.drain()
            s_dve.inc(v.engine_nop())
        dve_pb = s_dve.n

        mk3 = {"updone": [], "hdone": [], "dnadd": [], "gadd": [], "stores": [],
               "sil": [], "gath": [[], [], []], "meta": []}
        first = [True]

        for s3 in range(NS3):
            sl = s3 % 2
            @block.sync
            def _(sy, s3=s3, sl=sl):
                if len(mk3["stores"]) >= 2:
                    wait(sy, s_dve, mk3["meta"][-1])
                    sm, vv = mk3["stores"][-2]
                    wait(sy, sm, vv)
                s_ldx[sl].inc(sy.dma_start(out=yT_t[sl][:], in_=yTp_in[s3]), 16)
                s_ldx[sl].inc(sy.dma_start(
                    out=x2_t[sl][:].rearrange("p (a b) -> p a b", a=2),
                    in_=x2_in[s3 * 256:(s3 + 1) * 256, :]
                    .rearrange("(a p) b -> p a b", p=128)), 16)
                s_ldx[sl].inc(sy.dma_start(
                    out=idx_t[sl][:].rearrange("p (a b) -> p a b", a=2),
                    in_=idx_in[s3 * 256:(s3 + 1) * 256, :]
                    .rearrange("(a p) b -> p a b", p=128)), 16)
                s_ldx[sl].inc(sy.dma_start(
                    out=pos_t[sl][:].rearrange("p (a b) -> p a b", a=2),
                    in_=pos_in[s3 * 256:(s3 + 1) * 256, :]
                    .rearrange("(a p) b -> p a b", p=128)), 16)
            ldm = s_ldx[sl].n

            meta_marks2 = []
            for st in range(2):
                @block.vector
                def _(v, s3=s3, st=st, sl=sl):
                    wait(v, s_dve, dve_pb)
                    if st == 0:
                        wait(v, s_ldx[sl], ldm)
                    for k in range(TOPK):
                        v.tensor_scalar(out=eq6[:, k * E:(k + 1) * E], in0=iotaE[:],
                                        scalar1=idx_t[sl][:, st * TOPK + k:st * TOPK + k + 1],
                                        scalar2=None, op0=ALU.is_equal)
                    v.drain()
                    v.tensor_mul(out=eq6[:].rearrange("p (k e) -> p k e", k=TOPK),
                                 in0=eq6[:].rearrange("p (k e) -> p k e", k=TOPK),
                                 in1=prior_b[:].rearrange("p (o e) -> p o e", o=1)
                                 .broadcast_to([128, TOPK, E]))
                    v.drain()
                    v.reduce_sum(out=pk6[:].rearrange("p (k o) -> p k o", o=1),
                                 in_=eq6[:].rearrange("p (k e) -> p k e", k=TOPK),
                                 axis=AX.X)
                    v.drain()
                    v.tensor_add(out=gidx[:], in0=pk6[:],
                                 in1=pos_t[sl][:, st * TOPK:(st + 1) * TOPK])
                    v.drain()
                    v.tensor_scalar(out=okc[:], in0=gidx[:], scalar1=float(CAP) - 0.5,
                                    scalar2=None, op0=ALU.is_lt)
                    v.drain()
                    v.scalar_tensor_tensor(out=gidxb[:], in0=idx_t[sl]
                                           [:, st * TOPK:(st + 1) * TOPK],
                                           scalar=float(ESTRIDE), in1=gidx[:],
                                           op0=ALU.mult, op1=ALU.add)
                    v.drain()
                    v.tensor_scalar(out=gidxb[:], in0=gidxb[:], scalar1=-float(NZ),
                                    scalar2=None, op0=ALU.add)
                    v.drain()
                    v.tensor_mul(out=gidxb[:], in0=gidxb[:], in1=okc[:])
                    v.drain()
                    v.tensor_scalar(out=gidxb[:], in0=gidxb[:], scalar1=float(NZ),
                                    scalar2=None, op0=ALU.add)
                    v.drain()
                    v.tensor_copy(out=gidxu2[st][:], in_=gidxb[:])
                    v.drain()
                    s_dve.inc(v.engine_nop())
                meta_marks2.append(s_dve.n)
                mk3["meta"].append(s_dve.n)

            g_marks2 = []
            for st in range(2):
                tg = s3 * 2 + st
                gset = tg % 2
                @block.gpsimd
                def _(g, s3=s3, st=st, tg=tg, gset=gset):
                    wait(g, s_dve, meta_marks2[st])
                    if mk3["gath"][gset]:
                        wait(g, s_dve, mk3["gath"][gset][-1])
                    for k in range(TOPK):
                        s_gy[gset].inc(g.indirect_dma_start(
                            out=gbuf2[gset][k][:], out_offset=None, in_=y2z_in,
                            in_offset=IOA(gidxu2[st][:, k:k + 1], 0)), 16)
                g_marks2.append(s_gy[gset].n)

            pe_up = []
            @block.tensor
            def _(t, s3=s3, sl=sl):
                wait(t, s_ldx[sl], ldm)
                if first[0]:
                    wait(t, s_gpw, gpw_done)
                for hp in range(8):
                    if not first[0]:
                        if hp < 4:
                            wait(t, s_dve, mk3["hdone"][-1])
                        else:
                            wait(t, s_act, mk3["sil"][-1])
                    mm = None
                    for c in range(DCH):
                        mm = t.matmul(ps_su[hp],
                                      wsu_sb[:, c * 1024 + hp * 128:
                                             c * 1024 + (hp + 1) * 128],
                                      yT_t[sl][:, c * 256:(c + 1) * 256],
                                      start=(c == 0), stop=(c == DCH - 1))
                    pe_up.append(s_pe.inc(mm))
                first[0] = False
            mk3["updone"].append(s_pe.n)

            act_sil0 = s_act.n
            @block.scalar
            def _(s, s3=s3):
                if mk3["hdone"]:
                    wait(s, s_dve, mk3["hdone"][-1])
                for hc in range(4):
                    wait(s, s_pe, pe_up[4 + (hc | 1)])
                    s_act.inc(s.activation(out=sgs[:, hc * 256:(hc + 1) * 256],
                                           in_=ps_su[4 + hc], func=AF.Silu))
            mk3["sil"].append(s_act.n)

            @block.vector
            def _(v, s3=s3):
                if mk3["dnadd"]:
                    wait(v, s_pe, mk3["dnread"])
                for hc in range(4):
                    wait(v, s_act, act_sil0 + hc + 1)
                    wait(v, s_pe, pe_up[hc | 1])
                    s_dve.inc(v.tensor_mul(out=h_sb[:, hc * 256:(hc + 1) * 256],
                                           in0=sgs[:, hc * 256:(hc + 1) * 256],
                                           in1=ps_su[hc]))
            mk3["hdone"].append(s_dve.n)
            h_done = s_dve.n

            for st in range(2):
                tg = s3 * 2 + st
                asl = tg % 2
                pe_dn = []
                act_dn_pred = s_act.n if False else None
                dve_add0 = s_dve.n
                @block.tensor
                def _(t, s3=s3, st=st):
                    wait(t, s_dve, h_done)
                    for dg in range(4):
                        if dg >= 2:
                            wait(t, s_dve, dve_add0 + dg - 1)
                        elif mk3["dnadd"]:
                            wait(t, s_dve, mk3["dnadd"][-1] + dg - 1)
                        mm = None
                        for hc in range(4):
                            mm = t.matmul(
                                ps_dn[dg % 2][:],
                                h_sb[:, hc * 256 + st * 128:hc * 256 + st * 128 + 128],
                                wsd_sb[:, hc * 2048 + dg * 512:hc * 2048 + (dg + 1) * 512],
                                start=(hc == 0), stop=(hc == 3))
                        pe_dn.append(s_pe.inc(mm))
                mk3["dnread"] = s_pe.n

                @block.vector
                def _(v, s3=s3, st=st, sl=sl, asl=asl):
                    if len(mk3["stores"]) >= 2:
                        sm, vv = mk3["stores"][-2]
                        wait(v, sm, vv)
                    for dg in range(4):
                        wait(v, s_pe, pe_dn[dg])
                        s_dve.inc(v.tensor_add(
                            out=acc[asl][:, dg * 512:(dg + 1) * 512],
                            in0=ps_dn[dg % 2][:],
                            in1=x2_t[sl][:, st * 2048 + dg * 512:st * 2048 + (dg + 1) * 512]))
                mk3["dnadd"].append(s_dve.n)
                assert s_dve.n == dve_add0 + 4

                tg_ = s3 * 2 + st
                gset = tg_ % 2
                gb = gbuf2[gset]
                @block.vector
                def _(v, s3=s3, st=st, asl=asl, gset=gset, gb=gb):
                    wait(v, s_gy[gset], g_marks2[st])
                    v.tensor_add(out=gb[0][:], in0=gb[0][:], in1=gb[1][:])
                    v.tensor_add(out=gb[2][:], in0=gb[2][:], in1=gb[3][:])
                    v.tensor_add(out=gb[4][:], in0=gb[4][:], in1=gb[5][:])
                    v.drain()
                    v.tensor_add(out=gb[0][:], in0=gb[0][:], in1=gb[2][:])
                    v.drain()
                    wait(v, s_dve, mk3["dnadd"][-1])
                    v.tensor_add(out=acc[asl][:], in0=acc[asl][:], in1=gb[0][:])
                    v.drain()
                    v.tensor_add(out=acc[asl][:], in0=acc[asl][:], in1=gb[4][:])
                    v.drain()
                    s_dve.inc(v.engine_nop())
                ga_done = s_dve.n
                mk3["gath"][gset].append(ga_done)

                @block.scalar
                def _(s, s3=s3, st=st, tg=tg, asl=asl):
                    wait(s, s_dve, ga_done)
                    s_sto2[asl].inc(s.dma_start(out=out_o[tg * 128:(tg + 1) * 128, :],
                                                in_=acc[asl][:]), 16)
                mk3["stores"].append((s_sto2[asl], s_sto2[asl].n))

        @block.sync
        def _(sy):
            wait(sy, s_sto2[0])
            wait(sy, s_sto2[1])

    ctx.close()
    return nc



from concourse.bass_utils import run_bass_kernel_spmd

_NCORE = 8
_CFG = Cfg(S=4096, ST=512, CAP=1024)


def kernel(L0_attn_x, L0_attn_wq, L0_attn_wk, L0_attn_wv, L0_attn_wo,
           L0_ffn_w_gate, L0_ffn_w_shared_up, L0_ffn_w_shared_down,
           L0_ffn_w_up, L0_ffn_w_down):
    cfg = _CFG
    x = np.ascontiguousarray(np.asarray(L0_attn_x, dtype=np.float32))
    wq = np.asarray(L0_attn_wq, dtype=np.float32)
    wk = np.asarray(L0_attn_wk, dtype=np.float32)
    wv = np.asarray(L0_attn_wv, dtype=np.float32)
    wo = np.asarray(L0_attn_wo, dtype=np.float32)
    w_gate = np.asarray(L0_ffn_w_gate, dtype=np.float32)
    w_su = np.asarray(L0_ffn_w_shared_up, dtype=np.float32)
    w_sd = np.asarray(L0_ffn_w_shared_down, dtype=np.float32)
    w_up = np.asarray(L0_ffn_w_up, dtype=np.float32)
    w_dn = np.asarray(L0_ffn_w_down, dtype=np.float32)

    cores = list(range(_NCORE))

    nc1 = build_l1(cfg)
    im1 = [pack_l1_inputs(cfg, x[c], c, wq, wk, wv, wo, w_gate) for c in cores]
    r1 = run_bass_kernel_spmd(nc1, im1, core_ids=cores).results

    y_all = np.concatenate([r1[c]["y"] for c in cores], 0)
    tabs_all = np.concatenate([r1[c]["tabo"] for c in cores], 0)
    cnt_all = np.concatenate([r1[c]["cnto"] for c in cores], 0)

    nc2 = build_l2(cfg, ncore=_NCORE)
    im2 = [pack_l2_inputs(cfg, c, y_all, tabs_all, cnt_all, w_up, w_dn) for c in cores]
    r2 = run_bass_kernel_spmd(nc2, im2, core_ids=cores).results

    y2_all = np.concatenate([r2[c]["y2"] for c in cores], 0)

    nc3 = build_l3(cfg, ncore=_NCORE)
    im3 = [pack_l3_inputs(cfg, c, _NCORE, r1[c]["x2"], r1[c]["yTp"], r1[c]["idxo"],
                          r1[c]["poso"], cnt_all, y2_all, w_su, w_sd) for c in cores]
    r3 = run_bass_kernel_spmd(nc3, im3, core_ids=cores).results

    return np.stack([r3[c]["out"] for c in cores], 0)
